# version 02
# baseline (speedup 1.0000x reference)
import math
from contextlib import ExitStack

import numpy as np
import ml_dtypes

import concourse.bass as bass
import concourse.mybir as mybir
from concourse.bass_utils import run_bass_kernel_spmd

F32 = mybir.dt.float32
BF16 = mybir.dt.bfloat16
U32 = mybir.dt.uint32
AF = mybir.ActivationFunctionType
ALU = mybir.AluOpType
AX = mybir.AxisListType

BF16_NP = ml_dtypes.bfloat16

D = 1024
NCORES = 8
FULL_T = 8192
TT = 512
NH = 16
HD = 64
GSZ = 8



def make_consts(T):
    c = {}
    c["id128"] = np.eye(128, dtype=BF16_NP)
    e64 = np.eye(64, dtype=np.float32)
    c["id64x2"] = np.concatenate([e64, e64], axis=0).astype(BF16_NP)
    inv_freq = (1.0 / 10000.0 ** (np.arange(0, HD, 2, dtype=np.float32) / HD))
    t = np.arange(T, dtype=np.float32)
    fr = t[None, :] * inv_freq[:, None]
    cos = np.cos(fr).astype(np.float32)
    sin = np.sin(fr).astype(np.float32)
    C64 = np.concatenate([cos, cos], axis=0)
    S64 = np.concatenate([-sin, sin], axis=0)
    c["ropeC"] = np.concatenate([C64, C64], axis=0).astype(BF16_NP)
    c["ropeS"] = np.concatenate([S64, S64], axis=0).astype(BF16_NP)
    P = np.zeros((128, 128), dtype=BF16_NP)
    for m in range(128):
        P[(m // 64) * 64 + ((m % 64) ^ 32), m] = 1.0
    c["pswap"] = P
    Sel = np.zeros((128, 64), dtype=BF16_NP)
    for m in range(64):
        Sel[64 + m, m] = 1.0
    c["selhi"] = Sel
    Up = np.zeros((64, 128), dtype=BF16_NP)
    for d in range(64):
        Up[d, 64 + d] = 1.0
    c["selup"] = Up
    a = 32768.0
    Mfl = np.zeros((16, 128), dtype=BF16_NP)
    Mfr = np.zeros((16, 128), dtype=BF16_NP)
    Mfl[0, :] = a
    Mfr[0, :] = -a
    for r in range(GSZ):
        Mfl[1 + r, r::GSZ] = a
        Mfr[1 + r, r::GSZ] = a
    c["maskL"] = Mfl
    c["maskR"] = Mfr
    c["triuT"] = np.triu(np.ones((128, 128), dtype=np.float32)).astype(BF16_NP)
    c["iota32"] = np.tile(np.arange(32, dtype=np.float32)[None, :], (128, 1))
    c["ones128"] = np.ones((128, 1), dtype=BF16_NP)
    c["ones1x"] = np.ones((1, 128), dtype=BF16_NP)
    c["ones1xf"] = np.ones((1, 128), dtype=np.float32)
    return c



class Em:

    def __init__(self, nc, es):
        self.nc = nc
        self.es = es
        self._h = {}
        self._v = {}

    def eng(self, name):
        return getattr(self.nc, name)

    def sem(self, name):
        if name not in self._h:
            self._h[name] = self.es.enter_context(self.nc.semaphore(name))
            self._v[name] = 0
        return self._h[name]

    def val(self, name):
        self.sem(name)
        return self._v[name]

    def wait(self, eng, name, v=None):
        h = self.sem(name)
        v = self._v[name] if v is None else v
        if v > 0:
            self.eng(eng).wait_ge(h, v)

    def op(self, eng, fn, inc=None, n=1):
        ins = fn(self.eng(eng))
        if inc is not None:
            h = self.sem(inc)
            self._v[inc] += n
            ins.then_inc(h, n)
        self._last = getattr(self, "_last", {})
        self._last[eng] = (ins, inc)
        return ins

    def fence(self, eng):
        self._last = getattr(self, "_last", {})
        last = self._last.get(eng)
        if last is None:
            return
        ins, inc = last
        if inc is None:
            name = "fz_" + eng
            h = self.sem(name)
            self._v[name] += 1
            ins.then_inc(h, 1)
            self.eng(eng).wait_ge(h, self._v[name])
            self._last[eng] = (ins, name)
        else:
            self.eng(eng).wait_ge(self._h[inc], self._v[inc])

    def opf(self, eng, fn, inc=None, n=1):
        self.fence(eng)
        ins = self.op(eng, fn, inc=inc, n=n)
        self.fence(eng)
        return ins

    def dma(self, eng, out, in_, inc, **kw):
        h = self.sem(inc)
        self._v[inc] += 16
        ins = self.eng(eng).dma_start(out=out, in_=in_, **kw).then_inc(h, 16)
        self._last = getattr(self, "_last", {})
        self._last[eng] = (ins, inc)
        return ins

    def declare(self, names):
        for n in names:
            self.sem(n)

    def clear_and_reset(self):
        for n, h in self._h.items():
            self.nc.sync.sem_clear(h)
            self._v[n] = 0

    def loop_prologue(self):
        self.nc.all_engine_barrier()
        self.clear_and_reset()
        self.nc.all_engine_barrier()

    def end_iter(self):
        self.wait("sync", "st")
        self.nc.all_engine_barrier()
        self.clear_and_reset()
        self.nc.all_engine_barrier()


def off_ap(base, off):
    return bass.AP(base.tensor, off, base.ap)


def fm(ap, p=128):
    return ap.rearrange("(c p) t -> p c t", p=p)


def col(ap):
    return ap.rearrange("(c p) -> p c", p=128)


def g8(ap, s=GSZ):
    return ap.rearrange("p (g s) -> p g s", s=s)


def gslice(big, h, width=128):
    return big.rearrange("p (g s) -> p g s", s=width)[:, :, h * GSZ:(h + 1) * GSZ]



def emit_ln(em, bank, ctx, y_bf, out_b, nck=8, dmodel=1024.0):
    ones_s = ctx["ones_s"]; ones1x_s = ctx["ones1x_s"]
    y2_bf = ctx["y2_bf"]; srow = ctx["srow"]
    rstd_r = ctx["rstd_r"]; shift_r = ctx["shift_r"]
    TTl = y_bf.shape[1] // nck

    em.wait("tensor", "v")

    def s1_mm(t):
        for c in range(nck):
            i = t.matmul(bank(0)[0:1, 0:TTl], ones_s[:],
                         y_bf[:, c * TTl:(c + 1) * TTl],
                         start=(c == 0), stop=(c == nck - 1))
        return i
    em.op("tensor", s1_mm, inc="t")

    for c in range(nck):
        em.wait("scalar", "v")
        em.wait("scalar", "t")
        em.op("scalar",
              lambda s, c=c: s.activation(y2_bf[:], y_bf[:, c * TTl:(c + 1) * TTl],
                                          AF.Square),
              inc="s")
        em.wait("tensor", "s")
        em.op("tensor",
              lambda t, c=c: t.matmul(bank(1)[0:1, 0:TTl], ones_s[:], y2_bf[:],
                                      start=(c == 0), stop=(c == nck - 1)),
              inc="t")

    em.wait("vector", "t")

    def ln_rows(v):
        mean = srow[:, 0:TTl]
        e2 = srow[:, TTl:2 * TTl]
        var = srow[:, 2 * TTl:3 * TTl]
        v.tensor_scalar_mul(mean, bank(0)[0:1, 0:TTl], 1.0 / dmodel)
        v.tensor_scalar_mul(e2, bank(1)[0:1, 0:TTl], 1.0 / dmodel)
        v.tensor_mul(var, mean, mean)
        v.tensor_sub(var, e2, var)
        i = v.tensor_scalar_add(var, var, 1e-5)
        return i
    em.op("vector", ln_rows, inc="v")

    em.wait("scalar", "v")
    em.op("scalar",
          lambda s: s.activation(srow[:, 3 * TTl:4 * TTl],
                                 srow[:, 2 * TTl:3 * TTl], AF.Sqrt),
          inc="s")
    em.wait("vector", "s")

    rstd = srow[:, TTl:2 * TTl]
    em.opf("vector", lambda v: v.reciprocal(rstd, srow[:, 3 * TTl:4 * TTl]))

    def ln_fin(v):
        v.tensor_mul(srow[:, 2 * TTl:3 * TTl], srow[:, 0:TTl], rstd)
        v.tensor_copy(rstd_r[:, 0:TTl], rstd)
        i = v.tensor_copy(shift_r[:, 0:TTl], srow[:, 2 * TTl:3 * TTl])
        return i
    em.op("vector", ln_fin, inc="v")

    em.wait("tensor", "v")

    def ln_bc(t):
        t.matmul(bank(2)[:, 0:TTl], ones1x_s[:], rstd_r[:, 0:TTl],
                 start=True, stop=True)
        return t.matmul(bank(3)[:, 0:TTl], ones1x_s[:], shift_r[:, 0:TTl],
                        start=True, stop=True)
    em.op("tensor", ln_bc, inc="t")

    em.wait("vector", "t")

    def ln_apply(v):
        for c in range(nck):
            v.tensor_mul(out_b[:, c * TTl:(c + 1) * TTl],
                         y_bf[:, c * TTl:(c + 1) * TTl], bank(2)[:, 0:TTl])
            i = v.tensor_sub(out_b[:, c * TTl:(c + 1) * TTl],
                             out_b[:, c * TTl:(c + 1) * TTl], bank(3)[:, 0:TTl])
        return i
    em.op("vector", ln_apply, inc="v")



def _new_nc():
    return bass.Bass("TRN2", target_bir_lowering=False,
                     detect_race_conditions=False)


def build_l0attn(T, debug=False):
    nc = _new_nc()
    NTI = T // TT
    NG = TT // GSZ

    inp = lambda n, s, d=F32: nc.dram_tensor(n, s, d, kind="ExternalInput").ap()
    xT = inp("xT", [D, T])
    Wq = inp("Wq", [D, D]); bq = inp("bq", [D])
    Wdown = inp("Wdown", [D, 256]); bdown = inp("bdown", [256])
    Wup = inp("Wup", [256, 2048]); bup = inp("bup", [2048])
    Wout = inp("Wout", [D, D]); bout = inp("bout", [D])
    ropeC = inp("ropeC", [128, T], BF16)
    ropeS = inp("ropeS", [128, T], BF16)
    pswap = inp("pswap", [128, 128], BF16)
    selhi = inp("selhi", [128, 64], BF16)
    selup = inp("selup", [64, 128], BF16)
    maskL = inp("maskL", [16, 128], BF16)
    maskR = inp("maskR", [16, 128], BF16)
    id128 = inp("id128", [128, 128], BF16)
    id64x2 = inp("id64x2", [128, 64], BF16)
    ones128 = inp("ones128", [128, 1], BF16)
    ones1x = inp("ones1x", [1, 128], BF16)

    x1 = nc.dram_tensor("x1", [D, T], BF16, kind="ExternalOutput").ap()
    if debug:
        dbg_out = {
            nm: nc.dram_tensor("dbg_" + nm, shp, BF16, kind="ExternalOutput").ap()
            for nm, shp in [
                ("q_sb", [128, 8 * TT]), ("kvl_b", [128, 2 * TT]),
                ("qt_big", [64, NH * TT]), ("kt_big", [64, NH * TT]),
                ("vt_big", [128, NH * TT]), ("ot_big", [64, NH * TT]),
                ("o_flat", [128, 8 * TT]), ("y_bf", [128, 8 * TT]),
                ("x1_b", [128, 8 * TT]), ("p_sb", [128, 128]),
                ("pn_sb", [128, 128]), ("pt_sb", [128, 128]),
                ("vg_sb", [128, 64])]}
        dbg_out["nmax"] = nc.dram_tensor("dbg_nmax", [128, 1], F32,
                                         kind="ExternalOutput").ap()
        dbg_out["rsum"] = nc.dram_tensor("dbg_rsum", [128, 1], F32,
                                         kind="ExternalOutput").ap()
        dbg_out["rrec"] = nc.dram_tensor("dbg_rrec", [128, 1], F32,
                                         kind="ExternalOutput").ap()
        dbg_out["sdump"] = nc.dram_tensor("dbg_sdump", [128, 128], F32,
                                          kind="ExternalOutput").ap()

    es = ExitStack()
    sb = lambda n, s, d: es.enter_context(nc.sbuf_tensor(n, s, d))
    em = Em(nc, es)

    wq_b = sb("wq_b", [128, 8 * 1024], BF16)
    wdn_b = sb("wdn_b", [128, 8 * 256], BF16)
    wup_b = sb("wup_b", [128, 2 * 2048], BF16)
    wo_b = sb("wo_b", [128, 8 * 1024], BF16)
    stag = sb("stag", [128, 4 * 1024], F32)
    biasq = sb("biasq", [128, 8], F32)
    biasd = sb("biasd", [128, 2], F32)
    biasu = sb("biasu", [128, 16], F32)
    biaso = sb("biaso", [128, 8], F32)
    ropeC_s = sb("ropeC_s", [128, TT], BF16)
    ropeS_s = sb("ropeS_s", [128, TT], BF16)
    pswap_s = sb("pswap_s", [128, 128], BF16)
    selhi_s = sb("selhi_s", [128, 64], BF16)
    selup_s = sb("selup_s", [64, 128], BF16)
    maskL_s = sb("maskL_s", [16, 128], BF16)
    maskR_s = sb("maskR_s", [16, 128], BF16)
    id128_s = sb("id128_s", [128, 128], BF16)
    id64_s = sb("id64_s", [128, 64], BF16)
    ones_s = sb("ones_s", [128, 1], BF16)
    ones1x_s = sb("ones1x_s", [1, 128], BF16)

    xf = sb("xf", [128, 8 * TT], F32)
    xb = sb("xb", [128, 8 * TT], BF16)
    q_sb = sb("q_sb", [128, 8 * TT], BF16)
    qC = sb("qC", [128, TT], BF16)
    qS = sb("qS", [128, TT], BF16)
    kvl_b = sb("kvl_b", [128, 2 * TT], BF16)
    ktmp = sb("ktmp", [64, TT], BF16)
    kC = sb("kC", [64, TT], BF16)
    kS = sb("kS", [64, TT], BF16)
    qt_big = sb("qt_big", [64, NH * TT], BF16)
    kt_big = sb("kt_big", [64, NH * TT], BF16)
    vt_big = sb("vt_big", [128, NH * TT], BF16)
    ot_big = sb("ot_big", [64, NH * TT], BF16)
    o_flat = sb("o_flat", [128, 8 * TT], BF16)
    p_sb = sb("p_sb", [128, 128], BF16)
    pn_sb = sb("pn_sb", [128, 128], BF16)
    pt_sb = sb("pt_sb", [128, 128], BF16)
    vg_sb = sb("vg_sb", [128, 64], BF16)
    nmax = sb("nmax", [128, 1], F32)
    sdump = sb("sdump", [128, 128], F32)
    ebias = sb("ebias", [128, 1], F32)
    rsum = sb("rsum", [128, 1], F32)
    rrec = sb("rrec", [128, 1], F32)
    y_bf = sb("y_bf", [128, 8 * TT], BF16)
    y2_bf = sb("y2_bf", [128, TT], BF16)
    srow = sb("srow", [1, 4 * TT], F32)
    rstd_r = sb("rstd_r", [1, TT], BF16)
    shift_r = sb("shift_r", [1, TT], BF16)
    x1_b = sb("x1_b", [128, 8 * TT], BF16)

    pa = es.enter_context(nc.psum_tensor("pa", [128, 4096], F32))
    bank = lambda b: pa[:, 512 * b:512 * (b + 1)]
    bank_bf = lambda b, cols: pa.bitcast(BF16)[:, 1024 * b:1024 * b + cols]

    for dst, src in [(biasq, col(bq)), (biasd, col(bdown)), (biasu, col(bup)),
                     (biaso, col(bout))]:
        em.dma("sync", dst[:], src, "d", allow_slow_non_contiguous=True)
    for dst, src in [(pswap_s, pswap), (selhi_s, selhi), (selup_s, selup), (maskL_s, maskL),
                     (maskR_s, maskR), (id128_s, id128), (id64_s, id64x2),
                     (ones_s, ones128), (ones1x_s, ones1x)]:
        em.dma("sync", dst[:], src, "d")

    def cast_pass(dst_off, dst, n):
        def f(v):
            for c in range(n):
                i = v.tensor_copy(dst[:, dst_off + c * 1024:dst_off + (c + 1) * 1024],
                                  stag[:, c * 1024:(c + 1) * 1024])
            return i
        return f

    def load_weight(w_ap, dst, nchunks):
        K_, M_ = w_ap.shape
        nk = K_ // 128
        kb = max(1, 4096 // M_)
        view = fm(w_ap)
        k0 = 0
        off = 0
        while k0 < nk:
            n = min(kb, nk - k0)
            cols = n * M_
            em.wait("sync", "v")
            em.dma("sync", stag[:, 0:cols], view[:, k0:k0 + n, :], "d")
            em.wait("vector", "d")
            nch = cols // 1024
            em.op("vector", cast_pass(off, dst, nch), inc="v")
            k0 += n
            off += cols
    load_weight(Wq, wq_b, 8)
    load_weight(Wdown, wdn_b, 2)
    load_weight(Wup, wup_b, 4)
    load_weight(Wout, wo_b, 8)

    if debug:
        em.op("vector", lambda v: v.memset(vt_big[0:64, :], 0.0), inc="v")

    em.declare(["d", "v", "t", "s", "g", "st"])
    em.loop_prologue()
    fori_cm = nc.Fori(0, NTI)
    ti_r = fori_cm.__enter__()
    if True:
        toff = nc.sync.compute_val(ti_r * TT)

        em.dma("sync", xf[:], off_ap(fm(xT)[:, :, 0:TT], toff), "d")
        em.dma("sync", ropeC_s[:], off_ap(ropeC[:, 0:TT], toff), "d")
        em.dma("sync", ropeS_s[:], off_ap(ropeS[:, 0:TT], toff), "d")

        em.wait("vector", "d")

        def cast_x(v):
            for c in range(8):
                i = v.tensor_copy(xb[:, c * TT:(c + 1) * TT],
                                  xf[:, c * TT:(c + 1) * TT])
            return i
        em.op("vector", cast_x, inc="v")

        em.wait("tensor", "v")

        def q_mm(t):
            for m in range(8):
                for k in range(8):
                    i = t.matmul(
                        bank(m)[:, 0:TT],
                        wq_b[:, k * 1024 + m * 128:k * 1024 + (m + 1) * 128],
                        xb[:, k * TT:(k + 1) * TT],
                        start=(k == 0), stop=(k == 7))
            return i
        em.op("tensor", q_mm, inc="t")

        em.wait("vector", "t")

        def q_ev(v):
            for m in range(8):
                i = v.tensor_scalar_add(q_sb[:, m * TT:(m + 1) * TT],
                                        bank(m)[:, 0:TT], biasq[:, m:m + 1])
            return i
        em.op("vector", q_ev, inc="v")

        em.wait("tensor", "v")

        def kvl_mm(t):
            for m in range(2):
                for k in range(8):
                    i = t.matmul(
                        bank(m)[:, 0:TT],
                        wdn_b[:, k * 256 + m * 128:k * 256 + (m + 1) * 128],
                        xb[:, k * TT:(k + 1) * TT],
                        start=(k == 0), stop=(k == 7))
            return i
        em.op("tensor", kvl_mm, inc="t")

        em.wait("vector", "t")

        def kvl_ev(v):
            for m in range(2):
                i = v.tensor_scalar_add(kvl_b[:, m * TT:(m + 1) * TT],
                                        bank(m)[:, 0:TT], biasd[:, m:m + 1])
            return i
        em.op("vector", kvl_ev, inc="v")

        for c in range(8):
            h_even, h_odd = 2 * c, 2 * c + 1
            em.wait("tensor", "v")
            em.wait("tensor", "s")
            em.op("tensor",
                  lambda t, c=c: t.matmul(bank(2)[:, 0:TT], pswap_s[:],
                                          q_sb[:, c * TT:(c + 1) * TT],
                                          start=True, stop=True),
                  inc="t")
            em.wait("gpsimd", "v")
            em.wait("gpsimd", "d")
            em.op("gpsimd",
                  lambda g, c=c: g.tensor_mul(qC[:], q_sb[:, c * TT:(c + 1) * TT],
                                              ropeC_s[:]),
                  inc="g")
            em.wait("vector", "t")
            em.wait("vector", "g")

            def rope_q(v, h_even=h_even):
                v.tensor_mul(qS[:], bank(2)[:, 0:TT], ropeS_s[:])
                v.tensor_add(gslice(qt_big, h_even),
                             g8(qS[0:64, :]), g8(qC[0:64, :]))
                i = v.tensor_add(qS[64:128, :], qS[64:128, :], qC[64:128, :])
                return i
            em.op("vector", rope_q, inc="v")

            em.wait("tensor", "v")
            em.op("tensor",
                  lambda t: t.matmul(bank(2)[0:64, 0:TT], selhi_s[:], qS[:],
                                     start=True, stop=True),
                  inc="t")
            em.wait("scalar", "t")
            em.op("scalar",
                  lambda s, h_odd=h_odd: s.copy(
                      gslice(qt_big, h_odd),
                      g8(bank(2)[0:64, 0:TT])),
                  inc="s")

        for h in range(NH):
            em.wait("tensor", "v")
            em.wait("tensor", "s")
            em.wait("tensor", "g")

            def kv_mm(t, h=h):
                for k in range(2):
                    i = t.matmul(
                        bank(3)[:, 0:TT],
                        wup_b[:, k * 2048 + h * 128:k * 2048 + (h + 1) * 128],
                        kvl_b[:, k * TT:(k + 1) * TT],
                        start=(k == 0), stop=(k == 1))
                return i
            em.op("tensor", kv_mm, inc="t")

            em.wait("vector", "t")
            em.wait("vector", "g")

            def kv_ev(v, h=h):
                v.tensor_scalar_add(gslice(vt_big[64:128, :], h),
                                    g8(bank(3)[64:128, 0:TT]),
                                    biasu[64:128, h:h + 1])
                i = v.tensor_scalar_add(ktmp[:], bank(3)[0:64, 0:TT],
                                        biasu[0:64, h:h + 1])
                return i
            em.op("vector", kv_ev, inc="v")

            em.wait("tensor", "v")
            em.op("tensor",
                  lambda t: t.matmul(bank(2)[0:64, 0:TT], pswap_s[0:64, 0:64],
                                     ktmp[:], start=True, stop=True),
                  inc="t")
            em.wait("gpsimd", "v")
            em.op("gpsimd",
                  lambda g: g.tensor_mul(kC[:], ktmp[:], ropeC_s[0:64, :]),
                  inc="g")
            em.wait("vector", "t")
            em.wait("vector", "g")

            def rope_k(v, h=h):
                v.tensor_mul(kS[:], bank(2)[0:64, 0:TT], ropeS_s[0:64, :])
                i = v.tensor_add(gslice(kt_big, h), g8(kS[:]), g8(kC[:]))
                return i
            em.op("vector", rope_k, inc="v")

        for j in range(NG):
            em.wait("tensor", "v")
            em.wait("tensor", "s")

            def s_mm(t, j=j):
                t.matmul(bank(0)[:, 0:128],
                         qt_big[0:64, j * 128:(j + 1) * 128],
                         kt_big[0:64, j * 128:(j + 1) * 128],
                         start=True, stop=False)
                return t.matmul(bank(0)[:, 0:128], maskL_s[:], maskR_s[:],
                                start=False, stop=True)
            em.op("tensor", s_mm, inc="t")

            em.op("tensor",
                  lambda t, j=j: t.transpose(
                      bank_bf(5, 64),
                      vt_big[64:128, j * 128:(j + 1) * 128],
                      id64_s[64:128, :]),
                  inc="t")

            em.wait("vector", "t")
            em.wait("vector", "s")

            if debug and j == NG - 1:
                em.op("vector", lambda v: v.tensor_copy(sdump[:], bank(0)[:, 0:128]))
            em.opf("vector", lambda v: v.reduce_max(nmax[:], bank(0)[:, 0:128],
                                                    axis=AX.X, negate=True))
            em.op("vector", lambda v: v.tensor_scalar_mul(ebias[:], nmax[:], 0.125),
                  inc="v")

            em.wait("scalar", "v")
            em.op("scalar",
                  lambda s: s.activation(p_sb[:], bank(0)[:, 0:128], AF.Exp,
                                         scale=0.125, bias=ebias[:]),
                  inc="s")

            em.wait("vector", "s")

            em.opf("vector", lambda v: v.reduce_sum(rsum[:], p_sb[:], axis=AX.X))
            em.opf("vector", lambda v: v.reciprocal(rrec[:], rsum[:]))
            em.op("vector", lambda v: v.tensor_scalar_mul(pn_sb[:], p_sb[:], rrec[:]),
                  inc="v")

            em.wait("tensor", "v")
            em.op("tensor",
                  lambda t: t.transpose(bank_bf(4, 128), pn_sb[:], id128_s[:]),
                  inc="t")

            em.wait("scalar", "t")

            def evacs(s):
                s.copy(pt_sb[:], bank_bf(4, 128))
                return s.copy(vg_sb[:], bank_bf(5, 64))
            em.op("scalar", evacs, inc="s")

            em.wait("tensor", "s")
            em.op("tensor",
                  lambda t: t.matmul(bank(1)[0:64, 0:128], vg_sb[:], pt_sb[:],
                                     start=True, stop=True),
                  inc="t")

            em.wait("scalar", "t")
            em.op("scalar",
                  lambda s, j=j: s.copy(ot_big[0:64, j * 128:(j + 1) * 128],
                                        bank(1)[0:64, 0:128]),
                  inc="s")

        em.wait("gpsimd", "s")

        def oflat_g(g):
            for c in range(8):
                i = g.tensor_copy(o_flat[0:64, c * TT:(c + 1) * TT],
                                  gslice(ot_big, 2 * c))
            return i
        em.op("gpsimd", oflat_g, inc="g")

        em.wait("tensor", "v")
        for c in range(8):
            em.wait("tensor", "s")
            em.op("tensor",
                  lambda t, c=c: t.matmul(bank(2)[:, 0:TT], selup_s[:],
                                          gslice(ot_big, 2 * c + 1),
                                          start=True, stop=True),
                  inc="t")
            em.wait("scalar", "t")
            em.op("scalar",
                  lambda s, c=c: s.copy(o_flat[64:128, c * TT:(c + 1) * TT],
                                        bank(2)[64:128, 0:TT]),
                  inc="s")

        em.wait("tensor", "g")
        em.wait("tensor", "s")

        def wo_mm(t):
            for m in range(8):
                for k in range(8):
                    i = t.matmul(
                        bank(m)[:, 0:TT],
                        wo_b[:, k * 1024 + m * 128:k * 1024 + (m + 1) * 128],
                        o_flat[:, k * TT:(k + 1) * TT],
                        start=(k == 0), stop=(k == 7))
            return i
        em.op("tensor", wo_mm, inc="t")

        em.wait("vector", "t")

        def mk_y(v):
            for m in range(8):
                i = v.scalar_tensor_tensor(
                    y_bf[:, m * TT:(m + 1) * TT], bank(m)[:, 0:TT],
                    biaso[:, m:m + 1], xf[:, m * TT:(m + 1) * TT],
                    op0=ALU.add, op1=ALU.add)
            return i
        em.op("vector", mk_y, inc="v")

        ln_ctx = {"ones_s": ones_s, "ones1x_s": ones1x_s, "y2_bf": y2_bf,
                  "srow": srow, "rstd_r": rstd_r, "shift_r": shift_r}
        emit_ln(em, bank, ln_ctx, y_bf, x1_b)

        em.wait("sync", "v")
        em.dma("sync", off_ap(fm(x1)[:, :, 0:TT], toff), x1_b[:], "st")

        em.end_iter()
    fori_cm.__exit__(None, None, None)
    if debug:
        dbg_srcs = {"q_sb": q_sb, "kvl_b": kvl_b, "qt_big": qt_big,
                    "kt_big": kt_big, "vt_big": vt_big, "ot_big": ot_big,
                    "o_flat": o_flat, "y_bf": y_bf, "x1_b": x1_b,
                    "p_sb": p_sb, "pn_sb": pn_sb, "pt_sb": pt_sb,
                    "vg_sb": vg_sb, "nmax": nmax, "rsum": rsum, "rrec": rrec,
                    "sdump": sdump}
        for nm, ap_out in dbg_out.items():
            em.dma("sync", ap_out, dbg_srcs[nm][:], "st")
        em.wait("sync", "st")
    es.close()
    return nc





def build_l0ffn(T, debug=False):
    nc = _new_nc()
    SLAB = 512
    NSL = T // SLAB
    E = 16

    inp = lambda n, s, d=F32: nc.dram_tensor(n, s, d, kind="ExternalInput").ap()
    x1 = inp("x1", [D, T], BF16)
    gate_wT = inp("gate_wT", [D, E])
    w1 = inp("w1", [E, D, 512])
    w3 = inp("w3", [E, D, 512])
    w2 = inp("w2", [E, 512, D])
    id128 = inp("id128", [128, 128], BF16)
    iota32 = inp("iota32", [128, 32])
    ones128 = inp("ones128", [128, 1], BF16)
    ones1x = inp("ones1x", [1, 128], BF16)

    x2 = nc.dram_tensor("x2", [D, T], BF16, kind="ExternalOutput").ap()
    if debug:
        dbg_out = {}
        for nm, shp, dt_ in [
                ("scores_fmT", [16, 512], BF16), ("scr_row", [1, 16 * 512], BF16),
                ("sb_b", [128, 512], BF16), ("acc", [128, 8 * 512], F32),
                ("lgt", [128, 16], F32), ("prob", [128, 16], F32),
                ("scst_b", [128, 16], BF16), ("idx8f", [128, 8], F32),
                ("px1", [128, 16], F32), ("px2", [128, 16], F32),
                ("top8", [128, 8], F32), ("nmax", [128, 1], F32),
                ("rsum", [128, 1], F32), ("rrec", [128, 1], F32),
                ("selm", [128, 16], F32), ("y_bf", [128, 8 * 512], BF16),
                ("act_b", [128, 4 * 512], BF16), ("sg", [128, 4 * 512], BF16)]:
            dbg_out[nm] = nc.dram_tensor("dbg_" + nm, shp, dt_,
                                         kind="ExternalOutput").ap()

    es = ExitStack()
    sb = lambda n, s, d: es.enter_context(nc.sbuf_tensor(n, s, d))
    em = Em(nc, es)

    xb = sb("xb", [128, 8 * SLAB], BF16)
    gw_b = sb("gw_b", [128, 8 * E], BF16)
    w1_b = sb("w1_b", [128, 8 * 512], BF16)
    w3_b = sb("w3_b", [128, 8 * 512], BF16)
    w2_b = sb("w2_b", [128, 4 * 1024], BF16)
    stag = sb("stag", [128, 4 * 1024], F32)
    acc = sb("acc", [128, 8 * SLAB], F32)
    sg = sb("sg", [128, 4 * 512], BF16)
    act_b = sb("act_b", [128, 4 * 512], BF16)
    sb_b = sb("sb_b", [128, SLAB], BF16)
    lg_b = sb("lg_b", [16, SLAB], BF16)
    lgt = sb("lgt", [128, 16], F32)
    prob = sb("prob", [128, 16], F32)
    px1 = sb("px1", [128, 16], F32)
    px2 = sb("px2", [128, 16], F32)
    selm = sb("selm", [128, 16], F32)
    ohot = sb("ohot", [128, 16], F32)
    scst_b = sb("scst_b", [128, 16], BF16)
    top8 = sb("top8", [128, 8], F32)
    idx8u = sb("idx8u", [128, 8], U32)
    idx8f = sb("idx8f", [128, 8], F32)
    nmax = sb("nmax", [128, 1], F32)
    sdump = sb("sdump", [128, 128], F32)
    rsum = sb("rsum", [128, 1], F32)
    rrec = sb("rrec", [128, 1], F32)
    scores_fmT = sb("scores_fmT", [16, SLAB], BF16)
    scr_row = sb("scr_row", [1, E * SLAB], BF16)
    id128_s = sb("id128_s", [128, 128], BF16)
    iota_s = sb("iota_s", [128, 32], F32)
    ones_s = sb("ones_s", [128, 1], BF16)
    ones1x_s = sb("ones1x_s", [1, 128], BF16)
    y_bf = sb("y_bf", [128, 8 * SLAB], BF16)
    y2_bf = sb("y2_bf", [128, SLAB], BF16)
    srow = sb("srow", [1, 4 * SLAB], F32)
    rstd_r = sb("rstd_r", [1, SLAB], BF16)
    shift_r = sb("shift_r", [1, SLAB], BF16)
    x2_b = sb("x2_b", [128, 8 * SLAB], BF16)

    pa = es.enter_context(nc.psum_tensor("pa", [128, 4096], F32))
    bank = lambda b: pa[:, 512 * b:512 * (b + 1)]

    for dst, srcp in [(id128_s, id128), (iota_s, iota32), (ones_s, ones128),
                      (ones1x_s, ones1x)]:
        em.dma("sync", dst[:], srcp, "d")
    em.dma("sync", stag[:, 0:E * 8], fm(gate_wT), "d")
    em.wait("vector", "d")
    em.op("vector", lambda v: v.tensor_copy(gw_b[:], stag[:, 0:E * 8]), inc="v")

    ln_ctx = {"ones_s": ones_s, "ones1x_s": ones1x_s, "y2_bf": y2_bf,
              "srow": srow, "rstd_r": rstd_r, "shift_r": shift_r}

    em.declare(["d", "v", "t", "s", "g", "st"])
    em.loop_prologue()
    fori_cm = nc.Fori(0, NSL)
    sl_r = fori_cm.__enter__()
    if True:
        toff = nc.sync.compute_val(sl_r * SLAB)

        em.dma("sync", xb[:], off_ap(fm(x1)[:, :, 0:SLAB], toff), "d")

        em.wait("tensor", "d")
        em.wait("tensor", "v")

        def lg_mm(t):
            for k in range(8):
                i = t.matmul(bank(0)[0:16, 0:SLAB],
                             gw_b[:, k * E:(k + 1) * E],
                             xb[:, k * SLAB:(k + 1) * SLAB],
                             start=(k == 0), stop=(k == 7))
            return i
        em.op("tensor", lg_mm, inc="t")

        em.wait("scalar", "t")
        em.op("scalar", lambda s: s.copy(lg_b[:], bank(0)[0:16, 0:SLAB]), inc="s")

        for g in range(SLAB // 128):
            em.wait("tensor", "s")
            em.wait("tensor", "v")
            em.op("tensor",
                  lambda t, g=g: t.transpose(
                      pa.bitcast(BF16)[:, 1024:1024 + 16],
                      lg_b[:, g * 128:(g + 1) * 128], id128_s[0:16, 0:16]),
                  inc="t")
            em.wait("vector", "t")

            em.op("vector",
                  lambda v: v.tensor_copy(lgt[:], pa.bitcast(BF16)[:, 1024:1024 + 16]))
            em.opf("vector",
                   lambda v: v.reduce_max(nmax[:], lgt[:], axis=AX.X, negate=True),
                   inc="v")
            em.wait("scalar", "v")
            em.op("scalar",
                  lambda s: s.activation(prob[:], lgt[:], AF.Exp, bias=nmax[:]),
                  inc="s")
            em.wait("vector", "s")
            em.op("vector", lambda v: v.tensor_copy(px1[:], prob[:]))
            em.opf("vector", lambda v: v.reduce_sum(rsum[:], prob[:], axis=AX.X))
            em.opf("vector", lambda v: v.reciprocal(rrec[:], rsum[:]))
            em.opf("vector", lambda v: v.tensor_scalar_mul(prob[:], prob[:], rrec[:]))
            em.op("vector", lambda v: v.tensor_copy(px2[:], prob[:]))
            em.opf("vector", lambda v: v.max(top8[:], prob[:]))
            em.opf("vector", lambda v: v.max_index(idx8u[:], top8[:], prob[:]))
            em.opf("vector", lambda v: v.tensor_copy(idx8f[:], idx8u[:]))
            em.opf("vector", lambda v: v.memset(selm[:], 0.0))
            for kk in range(4):
                em.opf("vector",
                       lambda v, kk=kk: v.tensor_single_scalar(
                           ohot[:], iota_s[:, 0:16], idx8f[:, kk:kk + 1],
                           op=ALU.is_equal))
                em.opf("vector",
                       lambda v: v.tensor_add(selm[:], selm[:], ohot[:]))
            em.opf("vector", lambda v: v.tensor_mul(prob[:], prob[:], selm[:]))
            em.opf("vector", lambda v: v.reduce_sum(rsum[:], prob[:], axis=AX.X))
            em.opf("vector", lambda v: v.reciprocal(rrec[:], rsum[:]))
            em.opf("vector", lambda v: v.tensor_scalar_mul(scst_b[:], prob[:], rrec[:]),
                   inc="v")

            em.wait("tensor", "v")
            em.op("tensor",
                  lambda t: t.transpose(pa.bitcast(BF16)[:, 2048:2048 + 128][0:16, :],
                                        scst_b[:], id128_s[:]),
                  inc="t")
            em.wait("scalar", "t")
            em.op("scalar",
                  lambda s, g=g: s.copy(scores_fmT[:, g * 128:(g + 1) * 128],
                                        pa.bitcast(BF16)[:, 2048:2048 + 128][0:16, :]),
                  inc="s")

        em.wait("sync", "s")
        em.dma("sync", scr_row[:], scores_fmT[:], "d")

        em.op("vector", lambda v: [
            v.memset(acc[:, c * SLAB:(c + 1) * SLAB], 0.0) for c in range(8)][-1],
            inc="v")

        for e in range(E):
            for wsrc, wdst in [(w1[e], w1_b), (w3[e], w3_b), (w2[e], w2_b)]:
                em.wait("sync", "v")
                em.dma("sync", stag[:], fm(wsrc), "d")
                em.wait("vector", "d")

                def cst(v, wdst=wdst):
                    for c in range(4):
                        i = v.tensor_copy(wdst[:, c * 1024:(c + 1) * 1024],
                                          stag[:, c * 1024:(c + 1) * 1024])
                    return i
                em.op("vector", cst, inc="v")

            em.wait("tensor", "v")
            em.wait("tensor", "d")
            em.op("tensor",
                  lambda t, e=e: t.matmul(
                      bank(0)[:, 0:SLAB], ones1x_s[:],
                      scr_row[:, e * SLAB:e * SLAB + SLAB],
                      start=True, stop=True),
                  inc="t")
            em.wait("scalar", "t")
            em.op("scalar", lambda s: s.copy(sb_b[:], bank(0)[:, 0:SLAB]), inc="s")

            em.wait("tensor", "s")

            def gu_mm(t):
                for m in range(4):
                    for k in range(8):
                        t.matmul(bank(m)[:, 0:SLAB],
                                 w1_b[:, k * 512 + m * 128:k * 512 + (m + 1) * 128],
                                 xb[:, k * SLAB:(k + 1) * SLAB],
                                 start=(k == 0), stop=(k == 7))
                for m in range(4):
                    for k in range(8):
                        i = t.matmul(bank(4 + m)[:, 0:SLAB],
                                     w3_b[:, k * 512 + m * 128:k * 512 + (m + 1) * 128],
                                     xb[:, k * SLAB:(k + 1) * SLAB],
                                     start=(k == 0), stop=(k == 7))
                return i
            em.op("tensor", gu_mm, inc="t")

            em.wait("scalar", "t")

            def silu_f(s):
                for m in range(4):
                    i = s.activation(sg[:, m * 512:(m + 1) * 512],
                                     bank(m)[:, 0:SLAB], AF.Silu)
                return i
            em.op("scalar", silu_f, inc="s")

            em.wait("vector", "s")

            def mkact(v):
                for m in range(4):
                    v.tensor_mul(act_b[:, m * 512:(m + 1) * 512],
                                 bank(4 + m)[:, 0:SLAB], sb_b[:])
                    i = v.tensor_mul(act_b[:, m * 512:(m + 1) * 512],
                                     act_b[:, m * 512:(m + 1) * 512],
                                     sg[:, m * 512:(m + 1) * 512])
                return i
            em.op("vector", mkact, inc="v")

            em.wait("tensor", "v")

            def eo_mm(t):
                for m in range(8):
                    for k in range(4):
                        i = t.matmul(
                            bank(m)[:, 0:SLAB],
                            w2_b[:, k * 1024 + m * 128:k * 1024 + (m + 1) * 128],
                            act_b[:, k * 512:(k + 1) * 512],
                            start=(k == 0), stop=(k == 3))
                return i
            em.op("tensor", eo_mm, inc="t")

            em.wait("vector", "t")

            def flush(v):
                for m in range(8):
                    i = v.tensor_add(acc[:, m * SLAB:(m + 1) * SLAB],
                                     acc[:, m * SLAB:(m + 1) * SLAB],
                                     bank(m)[:, 0:SLAB])
                return i
            em.op("vector", flush, inc="v")

        def mk_y(v):
            for c in range(8):
                i = v.tensor_add(y_bf[:, c * SLAB:(c + 1) * SLAB],
                                 acc[:, c * SLAB:(c + 1) * SLAB],
                                 xb[:, c * SLAB:(c + 1) * SLAB])
            return i
        em.op("vector", mk_y, inc="v")

        emit_ln(em, bank, ln_ctx, y_bf, x2_b)
        em.wait("sync", "v")
        em.dma("sync", off_ap(fm(x2)[:, :, 0:SLAB], toff), x2_b[:], "st")

        em.end_iter()
    fori_cm.__exit__(None, None, None)
    if debug:
        dbg_srcs = {"scores_fmT": scores_fmT, "scr_row": scr_row, "sb_b": sb_b,
                    "acc": acc, "lgt": lgt, "prob": prob, "scst_b": scst_b,
                    "idx8f": idx8f, "selm": selm, "y_bf": y_bf,
                    "act_b": act_b, "sg": sg, "px1": px1, "px2": px2,
                    "top8": top8, "nmax": nmax, "rsum": rsum, "rrec": rrec}
        for nm, ap_out in dbg_out.items():
            em.dma("sync", ap_out, dbg_srcs[nm][:], "st")
        em.wait("sync", "st")
    es.close()
    return nc



def build_l1attn(T):
    nc = _new_nc()
    CH = 128
    NCH = T // CH

    inp = lambda n, s, d=F32: nc.dram_tensor(n, s, d, kind="ExternalInput").ap()
    x2 = inp("x2", [D, T], BF16)
    WqT = inp("WqT", [D, D]); bq = inp("bq", [D])
    WkT = inp("WkT", [D, D]); bk = inp("bk", [D])
    WvT = inp("WvT", [D, D]); bv = inp("bv", [D])
    WoT = inp("WoT", [D, D]); bo = inp("bo", [D])
    triuT = inp("triuT", [128, 128], BF16)
    id128 = inp("id128", [128, 128], BF16)
    ones128 = inp("ones128", [128, 1], BF16)
    ones1x = inp("ones1x", [1, 128], BF16)

    x3 = nc.dram_tensor("x3", [D, T], BF16, kind="ExternalOutput").ap()

    es = ExitStack()
    sb = lambda n, s, d: es.enter_context(nc.sbuf_tensor(n, s, d))
    em = Em(nc, es)

    wq_b = sb("wq_b", [128, 8 * 1024], BF16)
    wk_b = sb("wk_b", [128, 8 * 1024], BF16)
    wv_b = sb("wv_b", [128, 8 * 1024], BF16)
    wo_b = sb("wo_b", [128, 8 * 1024], BF16)
    stag = sb("stag", [128, 4 * 1024], F32)
    biasq = sb("biasq", [128, 8], F32)
    biask = sb("biask", [128, 8], F32)
    biasv = sb("biasv", [128, 8], F32)
    biaso = sb("biaso", [128, 8], F32)
    triu_s = sb("triu_s", [128, 128], BF16)
    id128_s = sb("id128_s", [128, 128], BF16)
    ones_s = sb("ones_s", [128, 1], BF16)
    ones1x_s = sb("ones1x_s", [1, 128], BF16)

    xb = sb("xb", [128, 8 * CH], BF16)
    q_fm = sb("q_fm", [128, 8 * CH], BF16)
    k_fm = sb("k_fm", [128, 8 * CH], BF16)
    v_fm = sb("v_fm", [128, 8 * CH], BF16)
    k_tm = sb("k_tm", [128, 8 * CH], BF16)
    v_tm = sb("v_tm", [128, 8 * CH], BF16)
    t1 = sb("t1", [128, CH], F32)
    t2 = sb("t2", [128, CH], F32)
    te = sb("te", [128, CH], F32)
    smT = sb("smT", [128, 128], BF16)
    dr = sb("dr", [128, 1], F32)
    sz_t = sb("sz_t", [128, 8 * 65], F32)
    sz_b = sb("sz_b", [128, 8 * 65], BF16)
    attn_tm = sb("attn_tm", [128, NH * HD], BF16)
    attn_fm = sb("attn_fm", [128, 8 * CH], BF16)
    y_bf = sb("y_bf", [128, 8 * CH], BF16)
    y2_bf = sb("y2_bf", [128, CH], BF16)
    srow = sb("srow", [1, 4 * CH], F32)
    rstd_r = sb("rstd_r", [1, CH], BF16)
    shift_r = sb("shift_r", [1, CH], BF16)
    x3_b = sb("x3_b", [128, 8 * CH], BF16)

    pa = es.enter_context(nc.psum_tensor("pa", [128, 4096], F32))
    bank = lambda b: pa[:, 512 * b:512 * (b + 1)]
    bank_bf = lambda b, cols: pa.bitcast(BF16)[:, 1024 * b:1024 * b + cols]

    for dst, srcp in [(biasq, col(bq)), (biask, col(bk)), (biasv, col(bv)),
                      (biaso, col(bo))]:
        em.dma("sync", dst[:], srcp, "d", allow_slow_non_contiguous=True)
    for dst, srcp in [(triu_s, triuT), (id128_s, id128), (ones_s, ones128),
                      (ones1x_s, ones1x)]:
        em.dma("sync", dst[:], srcp, "d")

    def cast_pass(dst_off, dst, n):
        def f(v):
            for c in range(n):
                i = v.tensor_copy(dst[:, dst_off + c * 1024:dst_off + (c + 1) * 1024],
                                  stag[:, c * 1024:(c + 1) * 1024])
            return i
        return f

    for w_ap, dst in [(WqT, wq_b), (WkT, wk_b), (WvT, wv_b), (WoT, wo_b)]:
        view = fm(w_ap)
        for k0 in range(0, 8, 4):
            em.wait("sync", "v")
            em.dma("sync", stag[:], view[:, k0:k0 + 4, :], "d")
            em.wait("vector", "d")
            em.op("vector", cast_pass(k0 * 1024, dst, 4), inc="v")

    def init_sz(v):
        v.memset(sz_t[:], 0.0)
        return v.memset(sz_b[:], 0.0)
    em.op("vector", init_sz, inc="v")

    ln_ctx = {"ones_s": ones_s, "ones1x_s": ones1x_s, "y2_bf": y2_bf,
              "srow": srow, "rstd_r": rstd_r, "shift_r": shift_r}

    def proj(name, wsb, biast, dst, do_phi):
        def mm(t, wsb=wsb):
            for m in range(8):
                bk_ = m // 4
                for k in range(8):
                    i = t.matmul(
                        bank(bk_)[:, (m % 4) * 128:(m % 4 + 1) * 128],
                        wsb[:, k * 1024 + m * 128:k * 1024 + (m + 1) * 128],
                        xb[:, k * CH:(k + 1) * CH],
                        start=(k == 0 and m % 4 == 0),
                        stop=(k == 7 and m % 4 == 3))
            return i
        em.op("tensor", mm, inc="t")
        for m in range(8):
            src = lambda m=m: bank(m // 4)[:, (m % 4) * 128:(m % 4 + 1) * 128]
            if do_phi:
                em.wait("vector", "t")
                em.op("vector",
                      lambda v, m=m, src=src: v.tensor_scalar(
                          t1[:], src(), biast[:, m:m + 1], 0.0,
                          op0=ALU.add, op1=ALU.min),
                      inc="v")
                em.wait("scalar", "v")
                em.op("scalar",
                      lambda s: s.activation(te[:], t1[:], AF.Exp), inc="s")
                em.wait("vector", "s")

                def phi_fin(v, m=m, src=src):
                    v.tensor_scalar(t2[:], src(), biast[:, m:m + 1], 0.0,
                                    op0=ALU.add, op1=ALU.max)
                    i = v.tensor_add(dst[:, m * CH:(m + 1) * CH], t2[:], te[:])
                    return i
                em.op("vector", phi_fin, inc="v")
            else:
                em.wait("vector", "t")
                em.op("vector",
                      lambda v, m=m, src=src: v.tensor_scalar_add(
                          dst[:, m * CH:(m + 1) * CH], src(), biast[:, m:m + 1]),
                      inc="v")

    em.declare(["d", "v", "t", "s", "g", "st"])
    em.loop_prologue()
    fori_cm = nc.Fori(0, NCH)
    ci_r = fori_cm.__enter__()
    if True:
        toff = nc.sync.compute_val(ci_r * CH)
        em.dma("sync", xb[:], off_ap(fm(x2)[:, :, 0:CH], toff), "d")
        em.wait("tensor", "d")
        em.wait("tensor", "v")
        em.wait("tensor", "s")

        proj("q", wq_b, biasq, q_fm, True)
        em.wait("tensor", "v")
        proj("k", wk_b, biask, k_fm, True)
        em.wait("tensor", "v")
        proj("v", wv_b, biasv, v_fm, False)

        em.wait("tensor", "v")
        for w2_ in range(2):
            for i4 in range(4):
                c = w2_ * 4 + i4
                em.op("tensor",
                      lambda t, c=c, i4=i4: t.transpose(
                          bank_bf(6, 1024)[:, i4 * 128:(i4 + 1) * 128],
                          k_fm[:, c * CH:(c + 1) * CH], id128_s[:]),
                      inc="t")
                em.op("tensor",
                      lambda t, c=c, i4=i4: t.transpose(
                          bank_bf(7, 1024)[:, i4 * 128:(i4 + 1) * 128],
                          v_fm[:, c * CH:(c + 1) * CH], id128_s[:]),
                      inc="t")
            em.wait("scalar", "t")

            def tev(s, w2_=w2_):
                for i4 in range(4):
                    c = w2_ * 4 + i4
                    s.copy(k_tm[:, c * CH:(c + 1) * CH],
                           bank_bf(6, 1024)[:, i4 * 128:(i4 + 1) * 128])
                    i = s.copy(v_tm[:, c * CH:(c + 1) * CH],
                               bank_bf(7, 1024)[:, i4 * 128:(i4 + 1) * 128])
                return i
            em.op("scalar", tev, inc="s")
            if w2_ == 0:
                em.wait("tensor", "s")

        for h in range(NH):
            hb = (h % 2) * 64
            hc = h // 2
            qh = q_fm[hb:hb + 64, hc * CH:(hc + 1) * CH]
            kh = k_fm[hb:hb + 64, hc * CH:(hc + 1) * CH]
            szh = sz_t[hb:hb + 64, (h // 2) * 65:(h // 2 + 1) * 65]
            szh_b = sz_b[hb:hb + 64, (h // 2) * 65:(h // 2 + 1) * 65]

            em.wait("tensor", "s")
            em.wait("tensor", "v")
            em.op("tensor",
                  lambda t, kh=kh, qh=qh: t.matmul(
                      bank(0)[:, 0:128], kh, qh, start=True, stop=True),
                  inc="t")
            em.wait("vector", "t")
            em.op("vector",
                  lambda v: v.tensor_mul(smT[:], bank(0)[:, 0:128], triu_s[:]),
                  inc="v")

            em.wait("tensor", "v")

            def num_mm(t, h=h, qh=qh, szh_b=szh_b):
                t.matmul(bank(1)[:, 0:64], smT[:],
                         v_tm[:, h * 64:(h + 1) * 64], start=True, stop=False)
                t.matmul(bank(1)[:, 64:65], smT[:], ones_s[:],
                         start=False, stop=False)
                i = t.matmul(bank(1)[:, 0:65], qh, szh_b, start=False, stop=True)
                return i
            em.op("tensor", num_mm, inc="t")

            em.wait("vector", "t")
            em.opf("vector",
                   lambda v: v.reciprocal(dr[:], bank(1)[:, 64:65]), inc="v")
            em.wait("scalar", "v")
            em.op("scalar",
                  lambda s, h=h: s.activation(
                      attn_tm[:, h * 64:(h + 1) * 64], bank(1)[:, 0:64],
                      AF.Copy, scale=dr[:]),
                  inc="s")

            def delta_mm(t, h=h, hb=hb):
                t.matmul(bank(2)[hb:hb + 64, 0:64],
                         k_tm[:, h * 64:(h + 1) * 64],
                         v_tm[:, h * 64:(h + 1) * 64], start=True, stop=False)
                i = t.matmul(bank(2)[hb:hb + 64, 64:65],
                             k_tm[:, h * 64:(h + 1) * 64], ones_s[:],
                             start=False, stop=True)
                return i
            em.op("tensor", delta_mm, inc="t")

            em.wait("vector", "t")

            em.op("vector",
                  lambda v, szh=szh, hb=hb: v.tensor_add(
                      szh, szh, bank(2)[hb:hb + 64, 0:65]))
            em.opf("vector",
                   lambda v, szh=szh, szh_b=szh_b: v.tensor_copy(szh_b, szh),
                   inc="v")

        em.wait("tensor", "s")
        em.wait("tensor", "v")
        for w2_ in range(2):
            for i4 in range(4):
                c = w2_ * 4 + i4
                em.op("tensor",
                      lambda t, c=c, i4=i4: t.transpose(
                          bank_bf(6, 1024)[:, i4 * 128:(i4 + 1) * 128],
                          attn_tm[:, c * CH:(c + 1) * CH], id128_s[:]),
                      inc="t")
            em.wait("scalar", "t")

            def tev2(s, w2_=w2_):
                for i4 in range(4):
                    c = w2_ * 4 + i4
                    i = s.copy(attn_fm[:, c * CH:(c + 1) * CH],
                               bank_bf(6, 1024)[:, i4 * 128:(i4 + 1) * 128])
                return i
            em.op("scalar", tev2, inc="s")
            if w2_ == 0:
                em.wait("tensor", "s")

        em.wait("tensor", "s")

        def wo_mm(t):
            for m in range(8):
                bk_ = m // 4
                for k in range(8):
                    i = t.matmul(
                        bank(bk_)[:, (m % 4) * 128:(m % 4 + 1) * 128],
                        wo_b[:, k * 1024 + m * 128:k * 1024 + (m + 1) * 128],
                        attn_fm[:, k * CH:(k + 1) * CH],
                        start=(k == 0 and m % 4 == 0),
                        stop=(k == 7 and m % 4 == 3))
            return i
        em.op("tensor", wo_mm, inc="t")

        em.wait("vector", "t")

        def mk_y(v):
            for m in range(8):
                i = v.scalar_tensor_tensor(
                    y_bf[:, m * CH:(m + 1) * CH],
                    bank(m // 4)[:, (m % 4) * 128:(m % 4 + 1) * 128],
                    biaso[:, m:m + 1], xb[:, m * CH:(m + 1) * CH],
                    op0=ALU.add, op1=ALU.add)
            return i
        em.op("vector", mk_y, inc="v")

        emit_ln(em, bank, ln_ctx, y_bf, x3_b)
        em.wait("sync", "v")
        em.dma("sync", off_ap(fm(x3)[:, :, 0:CH], toff), x3_b[:], "st")

        em.end_iter()
    fori_cm.__exit__(None, None, None)
    es.close()
    return nc



def build_l1ffn(T):
    nc = _new_nc()
    SLAB = 512
    NSL = T // SLAB
    E = 32

    inp = lambda n, s, d=F32: nc.dram_tensor(n, s, d, kind="ExternalInput").ap()
    x3 = inp("x3", [D, T], BF16)
    gate_wT = inp("gate_wT", [D, E])
    e_bias = inp("e_bias", [E])
    guT = inp("guT", [E, D, D])
    dnT = inp("dnT", [E, 512, D])
    sgT = inp("sgT", [D, D])
    suT = inp("suT", [D, D])
    sdT = inp("sdT", [D, D])
    id128 = inp("id128", [128, 128], BF16)
    iota32 = inp("iota32", [128, 32])
    ones128 = inp("ones128", [128, 1], BF16)
    ones1x = inp("ones1x", [1, 128], BF16)
    ones1xf = inp("ones1xf", [1, 128])

    x4 = nc.dram_tensor("x4", [D, T], F32, kind="ExternalOutput").ap()

    es = ExitStack()
    sb = lambda n, s, d: es.enter_context(nc.sbuf_tensor(n, s, d))
    em = Em(nc, es)

    xb = sb("xb", [128, 8 * SLAB], BF16)
    gw_b = sb("gw_b", [128, 8 * E], BF16)
    gu_b = sb("gu_b", [128, 8 * 1024], BF16)
    dn_b = sb("dn_b", [128, 4 * 1024], BF16)
    stag = sb("stag", [128, 4 * 1024], F32)
    acc = sb("acc", [128, 8 * SLAB], F32)
    sg = sb("sg", [128, 4 * 512], BF16)
    act_b = sb("act_b", [128, 4 * 512], BF16)
    sb_b = sb("sb_b", [128, SLAB], BF16)
    lg_b = sb("lg_b", [32, SLAB], BF16)
    lgt = sb("lgt", [128, 32], F32)
    scores = sb("scores", [128, 32], F32)
    sfc = sb("sfc", [128, 32], F32)
    gsc = sb("gsc", [128, 8], F32)
    gmask = sb("gmask", [128, 8], F32)
    masked = sb("masked", [128, 32], F32)
    selm = sb("selm", [128, 32], F32)
    ohot = sb("ohot", [128, 32], F32)
    scst_b = sb("scst_b", [128, 32], BF16)
    top8 = sb("top8", [128, 8], F32)
    idx8u = sb("idx8u", [128, 8], U32)
    idx8f = sb("idx8f", [128, 8], F32)
    rsum = sb("rsum", [128, 1], F32)
    rrec = sb("rrec", [128, 1], F32)
    ebias_row = sb("ebias_row", [1, E], F32)
    ebias_b = sb("ebias_b", [128, E], F32)
    scores_fmT = sb("scores_fmT", [32, SLAB], BF16)
    scr_row = sb("scr_row", [1, E * SLAB], BF16)
    id128_s = sb("id128_s", [128, 128], BF16)
    iota_s = sb("iota_s", [128, 32], F32)
    ones_s = sb("ones_s", [128, 1], BF16)
    ones1x_s = sb("ones1x_s", [1, 128], BF16)
    ones1xf_s = sb("ones1xf_s", [1, 128], F32)
    y_bf = sb("y_bf", [128, 8 * SLAB], BF16)
    y2_bf = sb("y2_bf", [128, SLAB], BF16)
    srow = sb("srow", [1, 4 * SLAB], F32)
    rstd_r = sb("rstd_r", [1, SLAB], BF16)
    shift_r = sb("shift_r", [1, SLAB], BF16)
    x4_f = sb("x4_f", [128, 8 * SLAB], F32)

    pa = es.enter_context(nc.psum_tensor("pa", [128, 4096], F32))
    bank = lambda b: pa[:, 512 * b:512 * (b + 1)]

    for dst, srcp in [(id128_s, id128), (iota_s, iota32), (ones_s, ones128),
                      (ones1x_s, ones1x), (ones1xf_s, ones1xf)]:
        em.dma("sync", dst[:], srcp, "d")
    em.dma("sync", ebias_row[:], e_bias.rearrange("(a b) -> a b", a=1), "d")
    em.dma("sync", stag[:, 0:E * 8], fm(gate_wT), "d")
    em.wait("vector", "d")
    em.op("vector", lambda v: v.tensor_copy(gw_b[:], stag[:, 0:E * 8]), inc="v")
    em.wait("tensor", "d")
    em.op("tensor",
          lambda t: t.matmul(bank(0)[:, 0:E], ones1xf_s[:], ebias_row[:],
                             start=True, stop=True),
          inc="t")
    em.wait("scalar", "t")
    em.op("scalar", lambda s: s.copy(ebias_b[:], bank(0)[:, 0:E]), inc="s")

    ln_ctx = {"ones_s": ones_s, "ones1x_s": ones1x_s, "y2_bf": y2_bf,
              "srow": srow, "rstd_r": rstd_r, "shift_r": shift_r}

    def cast_pass(dst_off, dst, n):
        def f(v):
            for c in range(n):
                i = v.tensor_copy(dst[:, dst_off + c * 1024:dst_off + (c + 1) * 1024],
                                  stag[:, c * 1024:(c + 1) * 1024])
            return i
        return f

    def stream_w(w_ap, dst, nk, koff=0):
        view = fm(w_ap)
        M_ = w_ap.shape[-1]
        per = 4096 // M_
        k0 = 0
        while k0 < nk:
            n = min(per, nk - k0)
            em.wait("sync", "v")
            em.dma("sync", stag[:, 0:n * M_], view[:, koff + k0:koff + k0 + n, :], "d")
            em.wait("vector", "d")
            em.op("vector", cast_pass(k0 * M_, dst, n * M_ // 1024), inc="v")
            k0 += n

    def expert_mms(lh_g, lh_u, lh_dn, use_score):
        em.wait("tensor", "s" if use_score else "v")
        em.wait("tensor", "v")

        def gu_mm(t):
            for m in range(4):
                for k in range(8):
                    t.matmul(bank(m)[:, 0:SLAB], lh_g(k, m),
                             xb[:, k * SLAB:(k + 1) * SLAB],
                             start=(k == 0), stop=(k == 7))
            for m in range(4):
                for k in range(8):
                    i = t.matmul(bank(4 + m)[:, 0:SLAB], lh_u(k, m),
                                 xb[:, k * SLAB:(k + 1) * SLAB],
                                 start=(k == 0), stop=(k == 7))
            return i
        em.op("tensor", gu_mm, inc="t")

        em.wait("scalar", "t")

        def silu_f(s):
            for m in range(4):
                i = s.activation(sg[:, m * 512:(m + 1) * 512],
                                 bank(m)[:, 0:SLAB], AF.Silu)
            return i
        em.op("scalar", silu_f, inc="s")

        em.wait("vector", "s")

        def mkact(v):
            for m in range(4):
                if use_score:
                    v.tensor_mul(act_b[:, m * 512:(m + 1) * 512],
                                 bank(4 + m)[:, 0:SLAB], sb_b[:])
                    i = v.tensor_mul(act_b[:, m * 512:(m + 1) * 512],
                                     act_b[:, m * 512:(m + 1) * 512],
                                     sg[:, m * 512:(m + 1) * 512])
                else:
                    i = v.tensor_mul(act_b[:, m * 512:(m + 1) * 512],
                                     bank(4 + m)[:, 0:SLAB],
                                     sg[:, m * 512:(m + 1) * 512])
            return i
        em.op("vector", mkact, inc="v")

        em.wait("tensor", "v")

        def eo_mm(t):
            for m in range(8):
                for k in range(4):
                    i = t.matmul(bank(m)[:, 0:SLAB], lh_dn(k, m),
                                 act_b[:, k * 512:(k + 1) * 512],
                                 start=(k == 0), stop=(k == 3))
            return i
        em.op("tensor", eo_mm, inc="t")

        em.wait("vector", "t")

        def flush(v):
            for m in range(8):
                i = v.tensor_add(acc[:, m * SLAB:(m + 1) * SLAB],
                                 acc[:, m * SLAB:(m + 1) * SLAB],
                                 bank(m)[:, 0:SLAB])
            return i
        em.op("vector", flush, inc="v")

    em.declare(["d", "v", "t", "s", "g", "st"])
    em.loop_prologue()
    fori_cm = nc.Fori(0, NSL)
    sl_r = fori_cm.__enter__()
    if True:
        toff = nc.sync.compute_val(sl_r * SLAB)
        em.dma("sync", xb[:], off_ap(fm(x3)[:, :, 0:SLAB], toff), "d")

        em.wait("tensor", "d")
        em.wait("tensor", "v")
        em.wait("tensor", "s")

        def lg_mm(t):
            for k in range(8):
                i = t.matmul(bank(0)[0:32, 0:SLAB], gw_b[:, k * E:(k + 1) * E],
                             xb[:, k * SLAB:(k + 1) * SLAB],
                             start=(k == 0), stop=(k == 7))
            return i
        em.op("tensor", lg_mm, inc="t")

        em.wait("scalar", "t")
        em.op("scalar", lambda s: s.copy(lg_b[:], bank(0)[0:32, 0:SLAB]), inc="s")

        for g in range(SLAB // 128):
            em.wait("tensor", "s")
            em.wait("tensor", "v")
            em.op("tensor",
                  lambda t, g=g: t.transpose(
                      pa.bitcast(BF16)[:, 1024:1024 + 32],
                      lg_b[:, g * 128:(g + 1) * 128], id128_s[0:32, 0:32]),
                  inc="t")
            em.wait("scalar", "t")
            em.op("scalar",
                  lambda s: s.activation(scores[:],
                                         pa.bitcast(BF16)[:, 1024:1024 + 32],
                                         AF.Sigmoid),
                  inc="s")
            em.wait("vector", "s")

            em.opf("vector", lambda v: v.tensor_add(sfc[:], scores[:], ebias_b[:]))
            em.opf("vector", lambda v: v.reduce_max(
                gsc[:], sfc[:].rearrange("p (a b) -> p a b", a=8), axis=AX.X))
            em.opf("vector", lambda v: v.max(top8[:], gsc[:]))
            em.opf("vector", lambda v: v.max_index(idx8u[:], top8[:], gsc[:]))
            em.opf("vector", lambda v: v.tensor_copy(idx8f[:], idx8u[:]))
            em.opf("vector", lambda v: v.memset(gmask[:], 0.0))
            for kk in range(4):
                em.opf("vector",
                       lambda v, kk=kk: v.tensor_single_scalar(
                           ohot[:, 0:8], iota_s[:, 0:8], idx8f[:, kk:kk + 1],
                           op=ALU.is_equal))
                em.opf("vector",
                       lambda v: v.tensor_add(gmask[:], gmask[:], ohot[:, 0:8]))
            em.opf("vector", lambda v: v.tensor_mul(
                masked[:].rearrange("p (a b) -> p a b", a=8),
                sfc[:].rearrange("p (a b) -> p a b", a=8),
                gmask[:, :, None].broadcast_to([128, 8, 4])), inc="v")

            em.opf("vector", lambda v: v.max(top8[:], masked[:]))
            em.opf("vector", lambda v: v.max_index(idx8u[:], top8[:], masked[:]))
            em.opf("vector", lambda v: v.tensor_copy(idx8f[:], idx8u[:]))
            em.opf("vector", lambda v: v.memset(selm[:], 0.0))
            for kk in range(6):
                em.opf("vector",
                       lambda v, kk=kk: v.tensor_single_scalar(
                           ohot[:], iota_s[:], idx8f[:, kk:kk + 1],
                           op=ALU.is_equal))
                em.opf("vector",
                       lambda v: v.tensor_add(selm[:], selm[:], ohot[:]))
            em.opf("vector", lambda v: v.tensor_mul(selm[:], selm[:], scores[:]))
            em.opf("vector", lambda v: v.reduce_sum(rsum[:], selm[:], axis=AX.X))
            em.opf("vector", lambda v: v.tensor_scalar_add(rsum[:], rsum[:], 1e-20))
            em.opf("vector", lambda v: v.reciprocal(rrec[:], rsum[:]))
            em.opf("vector", lambda v: v.tensor_scalar(scst_b[:], selm[:], rrec[:],
                                                       2.5, op0=ALU.mult, op1=ALU.mult),
                   inc="v")

            em.wait("tensor", "v")
            em.op("tensor",
                  lambda t: t.transpose(pa.bitcast(BF16)[:, 2048:2048 + 128][0:32, :],
                                        scst_b[:], id128_s[:]),
                  inc="t")
            em.wait("scalar", "t")
            em.op("scalar",
                  lambda s, g=g: s.copy(scores_fmT[:, g * 128:(g + 1) * 128],
                                        pa.bitcast(BF16)[:, 2048:2048 + 128][0:32, :]),
                  inc="s")

        em.wait("sync", "s")
        em.dma("sync", scr_row[:], scores_fmT[:], "d")

        em.op("vector", lambda v: [
            v.memset(acc[:, c * SLAB:(c + 1) * SLAB], 0.0) for c in range(8)][-1],
            inc="v")

        for half in range(2):
            em.wait("sync", "v")
            em.dma("sync", stag[:],
                   fm(sgT)[:, :, half * 512:(half + 1) * 512], "d")
            em.wait("vector", "d")
            em.op("vector", cast_pass(0, gu_b, 4), inc="v")
            em.wait("sync", "v")
            em.dma("sync", stag[:],
                   fm(suT)[:, :, half * 512:(half + 1) * 512], "d")
            em.wait("vector", "d")
            em.op("vector", cast_pass(4096, gu_b, 4), inc="v")
            stream_w(sdT, dn_b, 4, koff=half * 4)

            expert_mms(
                lambda k, m: gu_b[:, k * 512 + m * 128:k * 512 + (m + 1) * 128],
                lambda k, m: gu_b[:, 4096 + k * 512 + m * 128:4096 + k * 512 + (m + 1) * 128],
                lambda k, m: dn_b[:, k * 1024 + m * 128:k * 1024 + (m + 1) * 128],
                use_score=False)

        for e in range(E):
            stream_w(guT[e], gu_b, 8)
            stream_w(dnT[e], dn_b, 4)

            em.wait("tensor", "v")
            em.wait("tensor", "d")
            em.op("tensor",
                  lambda t, e=e: t.matmul(
                      bank(0)[:, 0:SLAB], ones1x_s[:],
                      scr_row[:, e * SLAB:e * SLAB + SLAB],
                      start=True, stop=True),
                  inc="t")
            em.wait("scalar", "t")
            em.op("scalar", lambda s: s.copy(sb_b[:], bank(0)[:, 0:SLAB]), inc="s")

            expert_mms(
                lambda k, m: gu_b[:, k * 1024 + m * 128:k * 1024 + (m + 1) * 128],
                lambda k, m: gu_b[:, k * 1024 + 512 + m * 128:k * 1024 + 512 + (m + 1) * 128],
                lambda k, m: dn_b[:, k * 1024 + m * 128:k * 1024 + (m + 1) * 128],
                use_score=True)

        def mk_y(v):
            for c in range(8):
                i = v.tensor_add(y_bf[:, c * SLAB:(c + 1) * SLAB],
                                 acc[:, c * SLAB:(c + 1) * SLAB],
                                 xb[:, c * SLAB:(c + 1) * SLAB])
            return i
        em.op("vector", mk_y, inc="v")

        emit_ln(em, bank, ln_ctx, y_bf, x4_f)
        em.wait("sync", "v")
        em.dma("sync", off_ap(fm(x4)[:, :, 0:SLAB], toff), x4_f[:], "st")

        em.end_iter()
    fori_cm.__exit__(None, None, None)
    es.close()
    return nc



def kernel(**inputs):
    T = inputs["L0_attn_x"].shape[1]
    cores = list(range(NCORES))
    c = make_consts(T)

    x = inputs["L0_attn_x"]
    xT = [np.ascontiguousarray(x[b].T) for b in range(NCORES)]

    nc1 = build_l0attn(T)
    common = {
        "Wq": inputs["L0_attn_Wq"], "bq": inputs["L0_attn_bq"],
        "Wdown": inputs["L0_attn_Wdown"], "bdown": inputs["L0_attn_bdown"],
        "Wup": inputs["L0_attn_Wup"], "bup": inputs["L0_attn_bup"],
        "Wout": inputs["L0_attn_Wout"], "bout": inputs["L0_attn_bout"],
        "ropeC": c["ropeC"], "ropeS": c["ropeS"], "pswap": c["pswap"],
        "selhi": c["selhi"], "selup": c["selup"], "maskL": c["maskL"],
        "maskR": c["maskR"], "id128": c["id128"], "id64x2": c["id64x2"],
        "ones128": c["ones128"], "ones1x": c["ones1x"],
    }
    r1 = run_bass_kernel_spmd(
        nc1, [dict(common, xT=xT[b]) for b in range(NCORES)], core_ids=cores)
    x1 = [np.asarray(r["x1"]) for r in r1.results]

    nc2 = build_l0ffn(T)
    common = {
        "gate_wT": np.ascontiguousarray(inputs["L0_ffn_gate_w"].T),
        "w1": inputs["L0_ffn_w1"], "w3": inputs["L0_ffn_w3"],
        "w2": inputs["L0_ffn_w2"],
        "id128": c["id128"], "iota32": c["iota32"],
        "ones128": c["ones128"], "ones1x": c["ones1x"],
    }
    r2 = run_bass_kernel_spmd(
        nc2, [dict(common, x1=x1[b]) for b in range(NCORES)], core_ids=cores)
    x2 = [np.asarray(r["x2"]) for r in r2.results]

    nc3 = build_l1attn(T)
    common = {
        "WqT": np.ascontiguousarray(inputs["L1_attn_Wq"].T), "bq": inputs["L1_attn_bq"],
        "WkT": np.ascontiguousarray(inputs["L1_attn_Wk"].T), "bk": inputs["L1_attn_bk"],
        "WvT": np.ascontiguousarray(inputs["L1_attn_Wv"].T), "bv": inputs["L1_attn_bv"],
        "WoT": np.ascontiguousarray(inputs["L1_attn_Wo"].T), "bo": inputs["L1_attn_bo"],
        "triuT": c["triuT"], "id128": c["id128"],
        "ones128": c["ones128"], "ones1x": c["ones1x"],
    }
    r3 = run_bass_kernel_spmd(
        nc3, [dict(common, x2=x2[b]) for b in range(NCORES)], core_ids=cores)
    x3 = [np.asarray(r["x3"]) for r in r3.results]

    nc4 = build_l1ffn(T)
    common = {
        "gate_wT": np.ascontiguousarray(inputs["L1_ffn_gate_weight"].T),
        "e_bias": inputs["L1_ffn_e_score_correction_bias"],
        "guT": np.ascontiguousarray(
            inputs["L1_ffn_experts_gate_up"].transpose(0, 2, 1)),
        "dnT": np.ascontiguousarray(
            inputs["L1_ffn_experts_down"].transpose(0, 2, 1)),
        "sgT": np.ascontiguousarray(inputs["L1_ffn_shared_gate_w"].T),
        "suT": np.ascontiguousarray(inputs["L1_ffn_shared_up_w"].T),
        "sdT": np.ascontiguousarray(inputs["L1_ffn_shared_down_w"].T),
        "id128": c["id128"], "iota32": c["iota32"], "ones128": c["ones128"],
        "ones1x": c["ones1x"], "ones1xf": c["ones1xf"],
    }
    r4 = run_bass_kernel_spmd(
        nc4, [dict(common, x3=x3[b]) for b in range(NCORES)], core_ids=cores)

    out = np.stack([np.asarray(r["x4"]).T for r in r4.results], axis=0)
    return np.ascontiguousarray(out)



# version 03
# speedup vs baseline: 1.0091x; 1.0091x over previous
import math
from contextlib import ExitStack

import numpy as np
import ml_dtypes

import concourse.bass as bass
import concourse.mybir as mybir
from concourse.bass_utils import run_bass_kernel_spmd

F32 = mybir.dt.float32
BF16 = mybir.dt.bfloat16
U32 = mybir.dt.uint32
AF = mybir.ActivationFunctionType
ALU = mybir.AluOpType
AX = mybir.AxisListType

BF16_NP = ml_dtypes.bfloat16

D = 1024
NCORES = 8
FULL_T = 8192
TT = 512
NH = 16
HD = 64
GSZ = 8



def make_consts(T):
    c = {}
    c["id128"] = np.eye(128, dtype=BF16_NP)
    e64 = np.eye(64, dtype=np.float32)
    c["id64x2"] = np.concatenate([e64, e64], axis=0).astype(BF16_NP)
    inv_freq = (1.0 / 10000.0 ** (np.arange(0, HD, 2, dtype=np.float32) / HD))
    t = np.arange(T, dtype=np.float32)
    fr = t[None, :] * inv_freq[:, None]
    cos = np.cos(fr).astype(np.float32)
    sin = np.sin(fr).astype(np.float32)
    C64 = np.concatenate([cos, cos], axis=0)
    S64 = np.concatenate([-sin, sin], axis=0)
    c["ropeC"] = np.concatenate([C64, C64], axis=0).astype(BF16_NP)
    c["ropeS"] = np.concatenate([S64, S64], axis=0).astype(BF16_NP)
    P = np.zeros((128, 128), dtype=BF16_NP)
    for m in range(128):
        P[(m // 64) * 64 + ((m % 64) ^ 32), m] = 1.0
    c["pswap"] = P
    Sel = np.zeros((128, 64), dtype=BF16_NP)
    for m in range(64):
        Sel[64 + m, m] = 1.0
    c["selhi"] = Sel
    Up = np.zeros((64, 128), dtype=BF16_NP)
    for d in range(64):
        Up[d, 64 + d] = 1.0
    c["selup"] = Up
    a = 32768.0
    Mfl = np.zeros((16, 128), dtype=BF16_NP)
    Mfr = np.zeros((16, 128), dtype=BF16_NP)
    Mfl[0, :] = a
    Mfr[0, :] = -a
    for r in range(GSZ):
        Mfl[1 + r, r::GSZ] = a
        Mfr[1 + r, r::GSZ] = a
    c["maskL"] = Mfl
    c["maskR"] = Mfr
    c["triuT"] = np.triu(np.ones((128, 128), dtype=np.float32)).astype(BF16_NP)
    c["iota32"] = np.tile(np.arange(32, dtype=np.float32)[None, :], (128, 1))
    c["ones128"] = np.ones((128, 1), dtype=BF16_NP)
    c["ones1x"] = np.ones((1, 128), dtype=BF16_NP)
    c["ones1xf"] = np.ones((1, 128), dtype=np.float32)
    return c



class Em:

    def __init__(self, nc, es):
        self.nc = nc
        self.es = es
        self._h = {}
        self._v = {}

    def eng(self, name):
        return getattr(self.nc, name)

    def sem(self, name):
        if name not in self._h:
            self._h[name] = self.es.enter_context(self.nc.semaphore(name))
            self._v[name] = 0
        return self._h[name]

    def val(self, name):
        self.sem(name)
        return self._v[name]

    def wait(self, eng, name, v=None):
        h = self.sem(name)
        v = self._v[name] if v is None else v
        if v > 0:
            self.eng(eng).wait_ge(h, v)

    def op(self, eng, fn, inc=None, n=1):
        ins = fn(self.eng(eng))
        if inc is not None:
            h = self.sem(inc)
            self._v[inc] += n
            ins.then_inc(h, n)
        self._last = getattr(self, "_last", {})
        self._last[eng] = (ins, inc)
        return ins

    def fence(self, eng):
        self._last = getattr(self, "_last", {})
        last = self._last.get(eng)
        if last is None:
            return
        ins, inc = last
        if inc is None:
            name = "fz_" + eng
            h = self.sem(name)
            self._v[name] += 1
            ins.then_inc(h, 1)
            self.eng(eng).wait_ge(h, self._v[name])
            self._last[eng] = (ins, name)
        else:
            self.eng(eng).wait_ge(self._h[inc], self._v[inc])

    def opf(self, eng, fn, inc=None, n=1):
        self.fence(eng)
        ins = self.op(eng, fn, inc=inc, n=n)
        self.fence(eng)
        return ins

    def dma(self, eng, out, in_, inc, **kw):
        h = self.sem(inc)
        self._v[inc] += 16
        ins = self.eng(eng).dma_start(out=out, in_=in_, **kw).then_inc(h, 16)
        self._last = getattr(self, "_last", {})
        self._last[eng] = (ins, inc)
        return ins

    def declare(self, names):
        for n in names:
            self.sem(n)

    def clear_and_reset(self):
        for n, h in self._h.items():
            self.nc.sync.sem_clear(h)
            self._v[n] = 0

    def loop_prologue(self):
        self.nc.all_engine_barrier()
        self.clear_and_reset()
        self.nc.all_engine_barrier()

    def end_iter(self):
        self.wait("sync", "st")
        self.nc.all_engine_barrier()
        self.clear_and_reset()
        self.nc.all_engine_barrier()


def off_ap(base, off):
    return bass.AP(base.tensor, off, base.ap)


def fm(ap, p=128):
    return ap.rearrange("(c p) t -> p c t", p=p)


def col(ap):
    return ap.rearrange("(c p) -> p c", p=128)


def g8(ap, s=GSZ):
    return ap.rearrange("p (g s) -> p g s", s=s)


def gslice(big, h, width=128):
    return big.rearrange("p (g s) -> p g s", s=width)[:, :, h * GSZ:(h + 1) * GSZ]



def emit_ln(em, bank, ctx, y_bf, out_b, nck=8, dmodel=1024.0):
    ones_s = ctx["ones_s"]; ones1x_s = ctx["ones1x_s"]
    y2_bf = ctx["y2_bf"]; srow = ctx["srow"]
    rstd_r = ctx["rstd_r"]; shift_r = ctx["shift_r"]
    TTl = y_bf.shape[1] // nck

    em.wait("tensor", "v")

    def s1_mm(t):
        for c in range(nck):
            i = t.matmul(bank(0)[0:1, 0:TTl], ones_s[:],
                         y_bf[:, c * TTl:(c + 1) * TTl],
                         start=(c == 0), stop=(c == nck - 1))
        return i
    em.op("tensor", s1_mm, inc="t")

    for c in range(nck):
        em.wait("scalar", "v")
        em.wait("scalar", "t")
        em.op("scalar",
              lambda s, c=c: s.activation(y2_bf[:], y_bf[:, c * TTl:(c + 1) * TTl],
                                          AF.Square),
              inc="s")
        em.wait("tensor", "s")
        em.op("tensor",
              lambda t, c=c: t.matmul(bank(1)[0:1, 0:TTl], ones_s[:], y2_bf[:],
                                      start=(c == 0), stop=(c == nck - 1)),
              inc="t")

    em.wait("vector", "t")

    def ln_rows(v):
        mean = srow[:, 0:TTl]
        e2 = srow[:, TTl:2 * TTl]
        var = srow[:, 2 * TTl:3 * TTl]
        v.tensor_scalar_mul(mean, bank(0)[0:1, 0:TTl], 1.0 / dmodel)
        v.tensor_scalar_mul(e2, bank(1)[0:1, 0:TTl], 1.0 / dmodel)
        v.tensor_mul(var, mean, mean)
        v.tensor_sub(var, e2, var)
        i = v.tensor_scalar_add(var, var, 1e-5)
        return i
    em.op("vector", ln_rows, inc="v")

    em.wait("scalar", "v")
    em.op("scalar",
          lambda s: s.activation(srow[:, 3 * TTl:4 * TTl],
                                 srow[:, 2 * TTl:3 * TTl], AF.Sqrt),
          inc="s")
    em.wait("vector", "s")

    rstd = srow[:, TTl:2 * TTl]
    em.opf("vector", lambda v: v.reciprocal(rstd, srow[:, 3 * TTl:4 * TTl]))

    def ln_fin(v):
        v.tensor_mul(srow[:, 2 * TTl:3 * TTl], srow[:, 0:TTl], rstd)
        v.tensor_copy(rstd_r[:, 0:TTl], rstd)
        i = v.tensor_copy(shift_r[:, 0:TTl], srow[:, 2 * TTl:3 * TTl])
        return i
    em.op("vector", ln_fin, inc="v")

    em.wait("tensor", "v")

    def ln_bc(t):
        t.matmul(bank(2)[:, 0:TTl], ones1x_s[:], rstd_r[:, 0:TTl],
                 start=True, stop=True)
        return t.matmul(bank(3)[:, 0:TTl], ones1x_s[:], shift_r[:, 0:TTl],
                        start=True, stop=True)
    em.op("tensor", ln_bc, inc="t")

    em.wait("vector", "t")

    def ln_apply(v):
        for c in range(nck):
            v.tensor_mul(out_b[:, c * TTl:(c + 1) * TTl],
                         y_bf[:, c * TTl:(c + 1) * TTl], bank(2)[:, 0:TTl])
            i = v.tensor_sub(out_b[:, c * TTl:(c + 1) * TTl],
                             out_b[:, c * TTl:(c + 1) * TTl], bank(3)[:, 0:TTl])
        return i
    em.op("vector", ln_apply, inc="v")



def _new_nc():
    return bass.Bass("TRN2", target_bir_lowering=False,
                     detect_race_conditions=False)


def emit_l0attn(nc, em, es, pa, A, T, pref="a_", debug=False):
    xT = A["xT"]
    Wq = A["Wq"]
    bq = A["bq"]
    Wdown = A["Wdown"]
    bdown = A["bdown"]
    Wup = A["Wup"]
    bup = A["bup"]
    Wout = A["Wout"]
    bout = A["bout"]
    ropeC = A["ropeC"]
    ropeS = A["ropeS"]
    pswap = A["pswap"]
    selhi = A["selhi"]
    selup = A["selup"]
    maskL = A["maskL"]
    maskR = A["maskR"]
    id128 = A["id128"]
    id64x2 = A["id64x2"]
    ones128 = A["ones128"]
    ones1x = A["ones1x"]
    x1 = A["x1"]
    dbg_out = A.get("dbg_out", {})
    bank = lambda b: pa[:, 512 * b:512 * (b + 1)]
    bank_bf = lambda b, cols: pa.bitcast(BF16)[:, 1024 * b:1024 * b + cols]
    NTI = T // TT
    NG = TT // GSZ
    sb = lambda n, s, d: es.enter_context(nc.sbuf_tensor(pref + n, s, d))

    wq_b = sb("wq_b", [128, 8 * 1024], BF16)
    wdn_b = sb("wdn_b", [128, 8 * 256], BF16)
    wup_b = sb("wup_b", [128, 2 * 2048], BF16)
    wo_b = sb("wo_b", [128, 8 * 1024], BF16)
    stag = sb("stag", [128, 4 * 1024], F32)
    biasq = sb("biasq", [128, 8], F32)
    biasd = sb("biasd", [128, 2], F32)
    biasu = sb("biasu", [128, 16], F32)
    biaso = sb("biaso", [128, 8], F32)
    ropeC_s = sb("ropeC_s", [128, TT], BF16)
    ropeS_s = sb("ropeS_s", [128, TT], BF16)
    pswap_s = sb("pswap_s", [128, 128], BF16)
    selhi_s = sb("selhi_s", [128, 64], BF16)
    selup_s = sb("selup_s", [64, 128], BF16)
    maskL_s = sb("maskL_s", [16, 128], BF16)
    maskR_s = sb("maskR_s", [16, 128], BF16)
    id128_s = sb("id128_s", [128, 128], BF16)
    id64_s = sb("id64_s", [128, 64], BF16)
    ones_s = sb("ones_s", [128, 1], BF16)
    ones1x_s = sb("ones1x_s", [1, 128], BF16)

    xf = sb("xf", [128, 8 * TT], F32)
    xb = sb("xb", [128, 8 * TT], BF16)
    q_sb = sb("q_sb", [128, 8 * TT], BF16)
    qC = sb("qC", [128, TT], BF16)
    qS = sb("qS", [128, TT], BF16)
    kvl_b = sb("kvl_b", [128, 2 * TT], BF16)
    ktmp = sb("ktmp", [64, TT], BF16)
    kC = sb("kC", [64, TT], BF16)
    kS = sb("kS", [64, TT], BF16)
    qt_big = sb("qt_big", [64, NH * TT], BF16)
    kt_big = sb("kt_big", [64, NH * TT], BF16)
    vt_big = sb("vt_big", [128, NH * TT], BF16)
    ot_big = sb("ot_big", [64, NH * TT], BF16)
    o_flat = sb("o_flat", [128, 8 * TT], BF16)
    p_sb = sb("p_sb", [128, 128], BF16)
    pn_sb = sb("pn_sb", [128, 128], BF16)
    pt_sb = sb("pt_sb", [128, 128], BF16)
    vg_sb = sb("vg_sb", [128, 64], BF16)
    nmax = sb("nmax", [128, 1], F32)
    sdump = sb("sdump", [128, 128], F32)
    ebias = sb("ebias", [128, 1], F32)
    rsum = sb("rsum", [128, 1], F32)
    rrec = sb("rrec", [128, 1], F32)
    y_bf = sb("y_bf", [128, 8 * TT], BF16)
    y2_bf = sb("y2_bf", [128, TT], BF16)
    srow = sb("srow", [1, 4 * TT], F32)
    rstd_r = sb("rstd_r", [1, TT], BF16)
    shift_r = sb("shift_r", [1, TT], BF16)
    x1_b = sb("x1_b", [128, 8 * TT], BF16)


    for dst, src in [(biasq, col(bq)), (biasd, col(bdown)), (biasu, col(bup)),
                     (biaso, col(bout))]:
        em.dma("sync", dst[:], src, "d", allow_slow_non_contiguous=True)
    for dst, src in [(pswap_s, pswap), (selhi_s, selhi), (selup_s, selup), (maskL_s, maskL),
                     (maskR_s, maskR), (id128_s, id128), (id64_s, id64x2),
                     (ones_s, ones128), (ones1x_s, ones1x)]:
        em.dma("sync", dst[:], src, "d")

    def cast_pass(dst_off, dst, n):
        def f(v):
            for c in range(n):
                i = v.tensor_copy(dst[:, dst_off + c * 1024:dst_off + (c + 1) * 1024],
                                  stag[:, c * 1024:(c + 1) * 1024])
            return i
        return f

    def load_weight(w_ap, dst, nchunks):
        K_, M_ = w_ap.shape
        nk = K_ // 128
        kb = max(1, 4096 // M_)
        view = fm(w_ap)
        k0 = 0
        off = 0
        while k0 < nk:
            n = min(kb, nk - k0)
            cols = n * M_
            em.wait("sync", "v")
            em.dma("sync", stag[:, 0:cols], view[:, k0:k0 + n, :], "d")
            em.wait("vector", "d")
            nch = cols // 1024
            em.op("vector", cast_pass(off, dst, nch), inc="v")
            k0 += n
            off += cols
    load_weight(Wq, wq_b, 8)
    load_weight(Wdown, wdn_b, 2)
    load_weight(Wup, wup_b, 4)
    load_weight(Wout, wo_b, 8)

    if debug:
        em.op("vector", lambda v: v.memset(vt_big[0:64, :], 0.0), inc="v")

    em.declare(["d", "v", "t", "s", "g", "st"])
    em.loop_prologue()
    fori_cm = nc.Fori(0, NTI)
    ti_r = fori_cm.__enter__()
    if True:
        toff = nc.sync.compute_val(ti_r * TT)

        em.dma("sync", xf[:], off_ap(fm(xT)[:, :, 0:TT], toff), "d")
        em.dma("sync", ropeC_s[:], off_ap(ropeC[:, 0:TT], toff), "d")
        em.dma("sync", ropeS_s[:], off_ap(ropeS[:, 0:TT], toff), "d")

        em.wait("vector", "d")

        def cast_x(v):
            for c in range(8):
                i = v.tensor_copy(xb[:, c * TT:(c + 1) * TT],
                                  xf[:, c * TT:(c + 1) * TT])
            return i
        em.op("vector", cast_x, inc="v")

        em.wait("tensor", "v")

        def q_mm(t):
            for m in range(8):
                for k in range(8):
                    i = t.matmul(
                        bank(m)[:, 0:TT],
                        wq_b[:, k * 1024 + m * 128:k * 1024 + (m + 1) * 128],
                        xb[:, k * TT:(k + 1) * TT],
                        start=(k == 0), stop=(k == 7))
            return i
        em.op("tensor", q_mm, inc="t")

        em.wait("vector", "t")

        def q_ev(v):
            for m in range(8):
                i = v.tensor_scalar_add(q_sb[:, m * TT:(m + 1) * TT],
                                        bank(m)[:, 0:TT], biasq[:, m:m + 1])
            return i
        em.op("vector", q_ev, inc="v")

        em.wait("tensor", "v")

        def kvl_mm(t):
            for m in range(2):
                for k in range(8):
                    i = t.matmul(
                        bank(m)[:, 0:TT],
                        wdn_b[:, k * 256 + m * 128:k * 256 + (m + 1) * 128],
                        xb[:, k * TT:(k + 1) * TT],
                        start=(k == 0), stop=(k == 7))
            return i
        em.op("tensor", kvl_mm, inc="t")

        em.wait("vector", "t")

        def kvl_ev(v):
            for m in range(2):
                i = v.tensor_scalar_add(kvl_b[:, m * TT:(m + 1) * TT],
                                        bank(m)[:, 0:TT], biasd[:, m:m + 1])
            return i
        em.op("vector", kvl_ev, inc="v")

        for c in range(8):
            h_even, h_odd = 2 * c, 2 * c + 1
            em.wait("tensor", "v")
            em.wait("tensor", "s")
            em.op("tensor",
                  lambda t, c=c: t.matmul(bank(2)[:, 0:TT], pswap_s[:],
                                          q_sb[:, c * TT:(c + 1) * TT],
                                          start=True, stop=True),
                  inc="t")
            em.wait("gpsimd", "v")
            em.wait("gpsimd", "d")
            em.op("gpsimd",
                  lambda g, c=c: g.tensor_mul(qC[:], q_sb[:, c * TT:(c + 1) * TT],
                                              ropeC_s[:]),
                  inc="g")
            em.wait("vector", "t")
            em.wait("vector", "g")

            def rope_q(v, h_even=h_even):
                v.tensor_mul(qS[:], bank(2)[:, 0:TT], ropeS_s[:])
                v.tensor_add(gslice(qt_big, h_even),
                             g8(qS[0:64, :]), g8(qC[0:64, :]))
                i = v.tensor_add(qS[64:128, :], qS[64:128, :], qC[64:128, :])
                return i
            em.op("vector", rope_q, inc="v")

            em.wait("tensor", "v")
            em.op("tensor",
                  lambda t: t.matmul(bank(2)[0:64, 0:TT], selhi_s[:], qS[:],
                                     start=True, stop=True),
                  inc="t")
            em.wait("scalar", "t")
            em.op("scalar",
                  lambda s, h_odd=h_odd: s.copy(
                      gslice(qt_big, h_odd),
                      g8(bank(2)[0:64, 0:TT])),
                  inc="s")

        for h in range(NH):
            em.wait("tensor", "v")
            em.wait("tensor", "s")
            em.wait("tensor", "g")

            def kv_mm(t, h=h):
                for k in range(2):
                    i = t.matmul(
                        bank(3)[:, 0:TT],
                        wup_b[:, k * 2048 + h * 128:k * 2048 + (h + 1) * 128],
                        kvl_b[:, k * TT:(k + 1) * TT],
                        start=(k == 0), stop=(k == 1))
                return i
            em.op("tensor", kv_mm, inc="t")

            em.wait("vector", "t")
            em.wait("vector", "g")

            def kv_ev(v, h=h):
                v.tensor_scalar_add(gslice(vt_big[64:128, :], h),
                                    g8(bank(3)[64:128, 0:TT]),
                                    biasu[64:128, h:h + 1])
                i = v.tensor_scalar_add(ktmp[:], bank(3)[0:64, 0:TT],
                                        biasu[0:64, h:h + 1])
                return i
            em.op("vector", kv_ev, inc="v")

            em.wait("tensor", "v")
            em.op("tensor",
                  lambda t: t.matmul(bank(2)[0:64, 0:TT], pswap_s[0:64, 0:64],
                                     ktmp[:], start=True, stop=True),
                  inc="t")
            em.wait("gpsimd", "v")
            em.op("gpsimd",
                  lambda g: g.tensor_mul(kC[:], ktmp[:], ropeC_s[0:64, :]),
                  inc="g")
            em.wait("vector", "t")
            em.wait("vector", "g")

            def rope_k(v, h=h):
                v.tensor_mul(kS[:], bank(2)[0:64, 0:TT], ropeS_s[0:64, :])
                i = v.tensor_add(gslice(kt_big, h), g8(kS[:]), g8(kC[:]))
                return i
            em.op("vector", rope_k, inc="v")

        for j in range(NG):
            em.wait("tensor", "v")
            em.wait("tensor", "s")

            def s_mm(t, j=j):
                t.matmul(bank(0)[:, 0:128],
                         qt_big[0:64, j * 128:(j + 1) * 128],
                         kt_big[0:64, j * 128:(j + 1) * 128],
                         start=True, stop=False)
                return t.matmul(bank(0)[:, 0:128], maskL_s[:], maskR_s[:],
                                start=False, stop=True)
            em.op("tensor", s_mm, inc="t")

            em.op("tensor",
                  lambda t, j=j: t.transpose(
                      bank_bf(5, 64),
                      vt_big[64:128, j * 128:(j + 1) * 128],
                      id64_s[64:128, :]),
                  inc="t")

            em.wait("vector", "t")
            em.wait("vector", "s")

            if debug and j == NG - 1:
                em.op("vector", lambda v: v.tensor_copy(sdump[:], bank(0)[:, 0:128]))
            em.opf("vector", lambda v: v.reduce_max(nmax[:], bank(0)[:, 0:128],
                                                    axis=AX.X, negate=True))
            em.op("vector", lambda v: v.tensor_scalar_mul(ebias[:], nmax[:], 0.125),
                  inc="v")

            em.wait("scalar", "v")
            em.op("scalar",
                  lambda s: s.activation(p_sb[:], bank(0)[:, 0:128], AF.Exp,
                                         scale=0.125, bias=ebias[:]),
                  inc="s")

            em.wait("vector", "s")

            em.opf("vector", lambda v: v.reduce_sum(rsum[:], p_sb[:], axis=AX.X))
            em.opf("vector", lambda v: v.reciprocal(rrec[:], rsum[:]))
            em.op("vector", lambda v: v.tensor_scalar_mul(pn_sb[:], p_sb[:], rrec[:]),
                  inc="v")

            em.wait("tensor", "v")
            em.op("tensor",
                  lambda t: t.transpose(bank_bf(4, 128), pn_sb[:], id128_s[:]),
                  inc="t")

            em.wait("scalar", "t")

            def evacs(s):
                s.copy(pt_sb[:], bank_bf(4, 128))
                return s.copy(vg_sb[:], bank_bf(5, 64))
            em.op("scalar", evacs, inc="s")

            em.wait("tensor", "s")
            em.op("tensor",
                  lambda t: t.matmul(bank(1)[0:64, 0:128], vg_sb[:], pt_sb[:],
                                     start=True, stop=True),
                  inc="t")

            em.wait("scalar", "t")
            em.op("scalar",
                  lambda s, j=j: s.copy(ot_big[0:64, j * 128:(j + 1) * 128],
                                        bank(1)[0:64, 0:128]),
                  inc="s")

        em.wait("gpsimd", "s")

        def oflat_g(g):
            for c in range(8):
                i = g.tensor_copy(o_flat[0:64, c * TT:(c + 1) * TT],
                                  gslice(ot_big, 2 * c))
            return i
        em.op("gpsimd", oflat_g, inc="g")

        em.wait("tensor", "v")
        for c in range(8):
            em.wait("tensor", "s")
            em.op("tensor",
                  lambda t, c=c: t.matmul(bank(2)[:, 0:TT], selup_s[:],
                                          gslice(ot_big, 2 * c + 1),
                                          start=True, stop=True),
                  inc="t")
            em.wait("scalar", "t")
            em.op("scalar",
                  lambda s, c=c: s.copy(o_flat[64:128, c * TT:(c + 1) * TT],
                                        bank(2)[64:128, 0:TT]),
                  inc="s")

        em.wait("tensor", "g")
        em.wait("tensor", "s")

        def wo_mm(t):
            for m in range(8):
                for k in range(8):
                    i = t.matmul(
                        bank(m)[:, 0:TT],
                        wo_b[:, k * 1024 + m * 128:k * 1024 + (m + 1) * 128],
                        o_flat[:, k * TT:(k + 1) * TT],
                        start=(k == 0), stop=(k == 7))
            return i
        em.op("tensor", wo_mm, inc="t")

        em.wait("vector", "t")

        def mk_y(v):
            for m in range(8):
                i = v.scalar_tensor_tensor(
                    y_bf[:, m * TT:(m + 1) * TT], bank(m)[:, 0:TT],
                    biaso[:, m:m + 1], xf[:, m * TT:(m + 1) * TT],
                    op0=ALU.add, op1=ALU.add)
            return i
        em.op("vector", mk_y, inc="v")

        ln_ctx = {"ones_s": ones_s, "ones1x_s": ones1x_s, "y2_bf": y2_bf,
                  "srow": srow, "rstd_r": rstd_r, "shift_r": shift_r}
        emit_ln(em, bank, ln_ctx, y_bf, x1_b)

        em.wait("sync", "v")
        em.dma("sync", off_ap(fm(x1)[:, :, 0:TT], toff), x1_b[:], "st")

        em.end_iter()
    fori_cm.__exit__(None, None, None)
    if debug:
        dbg_srcs = {"q_sb": q_sb, "kvl_b": kvl_b, "qt_big": qt_big,
                    "kt_big": kt_big, "vt_big": vt_big, "ot_big": ot_big,
                    "o_flat": o_flat, "y_bf": y_bf, "x1_b": x1_b,
                    "p_sb": p_sb, "pn_sb": pn_sb, "pt_sb": pt_sb,
                    "vg_sb": vg_sb, "nmax": nmax, "rsum": rsum, "rrec": rrec,
                    "sdump": sdump}
        for nm, ap_out in dbg_out.items():
            em.dma("sync", ap_out, dbg_srcs[nm][:], "st")
        em.wait("sync", "st")



def build_l0attn(T, debug=False):
    nc = _new_nc()
    NTI = T // TT
    NG = TT // GSZ

    inp = lambda n, s, d=F32: nc.dram_tensor(n, s, d, kind="ExternalInput").ap()
    xT = inp("xT", [D, T])
    Wq = inp("Wq", [D, D]); bq = inp("bq", [D])
    Wdown = inp("Wdown", [D, 256]); bdown = inp("bdown", [256])
    Wup = inp("Wup", [256, 2048]); bup = inp("bup", [2048])
    Wout = inp("Wout", [D, D]); bout = inp("bout", [D])
    ropeC = inp("ropeC", [128, T], BF16)
    ropeS = inp("ropeS", [128, T], BF16)
    pswap = inp("pswap", [128, 128], BF16)
    selhi = inp("selhi", [128, 64], BF16)
    selup = inp("selup", [64, 128], BF16)
    maskL = inp("maskL", [16, 128], BF16)
    maskR = inp("maskR", [16, 128], BF16)
    id128 = inp("id128", [128, 128], BF16)
    id64x2 = inp("id64x2", [128, 64], BF16)
    ones128 = inp("ones128", [128, 1], BF16)
    ones1x = inp("ones1x", [1, 128], BF16)

    x1 = nc.dram_tensor("x1", [D, T], BF16, kind="ExternalOutput").ap()
    if debug:
        dbg_out = {
            nm: nc.dram_tensor("dbg_" + nm, shp, BF16, kind="ExternalOutput").ap()
            for nm, shp in [
                ("q_sb", [128, 8 * TT]), ("kvl_b", [128, 2 * TT]),
                ("qt_big", [64, NH * TT]), ("kt_big", [64, NH * TT]),
                ("vt_big", [128, NH * TT]), ("ot_big", [64, NH * TT]),
                ("o_flat", [128, 8 * TT]), ("y_bf", [128, 8 * TT]),
                ("x1_b", [128, 8 * TT]), ("p_sb", [128, 128]),
                ("pn_sb", [128, 128]), ("pt_sb", [128, 128]),
                ("vg_sb", [128, 64])]}
        dbg_out["nmax"] = nc.dram_tensor("dbg_nmax", [128, 1], F32,
                                         kind="ExternalOutput").ap()
        dbg_out["rsum"] = nc.dram_tensor("dbg_rsum", [128, 1], F32,
                                         kind="ExternalOutput").ap()
        dbg_out["rrec"] = nc.dram_tensor("dbg_rrec", [128, 1], F32,
                                         kind="ExternalOutput").ap()
        dbg_out["sdump"] = nc.dram_tensor("dbg_sdump", [128, 128], F32,
                                          kind="ExternalOutput").ap()

    es = ExitStack()
    em = Em(nc, es)
    pa = es.enter_context(nc.psum_tensor("pa", [128, 4096], F32))
    A = {"xT": xT, "Wq": Wq, "bq": bq, "Wdown": Wdown, "bdown": bdown, "Wup": Wup, "bup": bup, "Wout": Wout, "bout": bout, "ropeC": ropeC, "ropeS": ropeS, "pswap": pswap, "selhi": selhi, "selup": selup, "maskL": maskL, "maskR": maskR, "id128": id128, "id64x2": id64x2, "ones128": ones128, "ones1x": ones1x, "x1": x1}
    if debug:
        A["dbg_out"] = dbg_out
    emit_l0attn(nc, em, es, pa, A, T, debug=debug)
    es.close()
    return nc



def emit_l0ffn(nc, em, es, pa, A, T, pref="b_", debug=False):
    x1 = A["x1"]
    gate_wT = A["gate_wT"]
    w1 = A["w1"]
    w3 = A["w3"]
    w2 = A["w2"]
    id128 = A["id128"]
    iota32 = A["iota32"]
    ones128 = A["ones128"]
    ones1x = A["ones1x"]
    x2 = A["x2"]
    dbg_out = A.get("dbg_out", {})
    bank = lambda b: pa[:, 512 * b:512 * (b + 1)]
    bank_bf = lambda b, cols: pa.bitcast(BF16)[:, 1024 * b:1024 * b + cols]
    SLAB = 512
    NSL = T // SLAB
    E = 16
    sb = lambda n, s, d: es.enter_context(nc.sbuf_tensor(pref + n, s, d))

    xb = sb("xb", [128, 8 * SLAB], BF16)
    gw_b = sb("gw_b", [128, 8 * E], BF16)
    w1_b = sb("w1_b", [128, 8 * 512], BF16)
    w3_b = sb("w3_b", [128, 8 * 512], BF16)
    w2_b = sb("w2_b", [128, 4 * 1024], BF16)
    stag = sb("stag", [128, 4 * 1024], F32)
    acc = sb("acc", [128, 8 * SLAB], F32)
    sg = sb("sg", [128, 4 * 512], BF16)
    act_b = sb("act_b", [128, 4 * 512], BF16)
    sb_b = sb("sb_b", [128, SLAB], BF16)
    lg_b = sb("lg_b", [16, SLAB], BF16)
    lgt = sb("lgt", [128, 16], F32)
    prob = sb("prob", [128, 16], F32)
    px1 = sb("px1", [128, 16], F32)
    px2 = sb("px2", [128, 16], F32)
    selm = sb("selm", [128, 16], F32)
    ohot = sb("ohot", [128, 16], F32)
    scst_b = sb("scst_b", [128, 16], BF16)
    top8 = sb("top8", [128, 8], F32)
    idx8u = sb("idx8u", [128, 8], U32)
    idx8f = sb("idx8f", [128, 8], F32)
    nmax = sb("nmax", [128, 1], F32)
    sdump = sb("sdump", [128, 128], F32)
    rsum = sb("rsum", [128, 1], F32)
    rrec = sb("rrec", [128, 1], F32)
    scores_fmT = sb("scores_fmT", [16, SLAB], BF16)
    scr_row = sb("scr_row", [1, E * SLAB], BF16)
    id128_s = sb("id128_s", [128, 128], BF16)
    iota_s = sb("iota_s", [128, 32], F32)
    ones_s = sb("ones_s", [128, 1], BF16)
    ones1x_s = sb("ones1x_s", [1, 128], BF16)
    y_bf = sb("y_bf", [128, 8 * SLAB], BF16)
    y2_bf = sb("y2_bf", [128, SLAB], BF16)
    srow = sb("srow", [1, 4 * SLAB], F32)
    rstd_r = sb("rstd_r", [1, SLAB], BF16)
    shift_r = sb("shift_r", [1, SLAB], BF16)
    x2_b = sb("x2_b", [128, 8 * SLAB], BF16)


    for dst, srcp in [(id128_s, id128), (iota_s, iota32), (ones_s, ones128),
                      (ones1x_s, ones1x)]:
        em.dma("sync", dst[:], srcp, "d")
    em.dma("sync", stag[:, 0:E * 8], fm(gate_wT), "d")
    em.wait("vector", "d")
    em.op("vector", lambda v: v.tensor_copy(gw_b[:], stag[:, 0:E * 8]), inc="v")

    ln_ctx = {"ones_s": ones_s, "ones1x_s": ones1x_s, "y2_bf": y2_bf,
              "srow": srow, "rstd_r": rstd_r, "shift_r": shift_r}

    em.declare(["d", "v", "t", "s", "g", "st"])
    em.loop_prologue()
    fori_cm = nc.Fori(0, NSL)
    sl_r = fori_cm.__enter__()
    if True:
        toff = nc.sync.compute_val(sl_r * SLAB)

        em.dma("sync", xb[:], off_ap(fm(x1)[:, :, 0:SLAB], toff), "d")

        em.wait("tensor", "d")
        em.wait("tensor", "v")

        def lg_mm(t):
            for k in range(8):
                i = t.matmul(bank(0)[0:16, 0:SLAB],
                             gw_b[:, k * E:(k + 1) * E],
                             xb[:, k * SLAB:(k + 1) * SLAB],
                             start=(k == 0), stop=(k == 7))
            return i
        em.op("tensor", lg_mm, inc="t")

        em.wait("scalar", "t")
        em.op("scalar", lambda s: s.copy(lg_b[:], bank(0)[0:16, 0:SLAB]), inc="s")

        for g in range(SLAB // 128):
            em.wait("tensor", "s")
            em.wait("tensor", "v")
            em.op("tensor",
                  lambda t, g=g: t.transpose(
                      pa.bitcast(BF16)[:, 1024:1024 + 16],
                      lg_b[:, g * 128:(g + 1) * 128], id128_s[0:16, 0:16]),
                  inc="t")
            em.wait("vector", "t")

            em.op("vector",
                  lambda v: v.tensor_copy(lgt[:], pa.bitcast(BF16)[:, 1024:1024 + 16]))
            em.opf("vector",
                   lambda v: v.reduce_max(nmax[:], lgt[:], axis=AX.X, negate=True),
                   inc="v")
            em.wait("scalar", "v")
            em.op("scalar",
                  lambda s: s.activation(prob[:], lgt[:], AF.Exp, bias=nmax[:]),
                  inc="s")
            em.wait("vector", "s")
            em.op("vector", lambda v: v.tensor_copy(px1[:], prob[:]))
            em.opf("vector", lambda v: v.reduce_sum(rsum[:], prob[:], axis=AX.X))
            em.opf("vector", lambda v: v.reciprocal(rrec[:], rsum[:]))
            em.opf("vector", lambda v: v.tensor_scalar_mul(prob[:], prob[:], rrec[:]))
            em.op("vector", lambda v: v.tensor_copy(px2[:], prob[:]))
            em.opf("vector", lambda v: v.max(top8[:], prob[:]))
            em.opf("vector", lambda v: v.max_index(idx8u[:], top8[:], prob[:]))
            em.opf("vector", lambda v: v.tensor_copy(idx8f[:], idx8u[:]))
            em.opf("vector", lambda v: v.memset(selm[:], 0.0))
            for kk in range(4):
                em.opf("vector",
                       lambda v, kk=kk: v.tensor_single_scalar(
                           ohot[:], iota_s[:, 0:16], idx8f[:, kk:kk + 1],
                           op=ALU.is_equal))
                em.opf("vector",
                       lambda v: v.tensor_add(selm[:], selm[:], ohot[:]))
            em.opf("vector", lambda v: v.tensor_mul(prob[:], prob[:], selm[:]))
            em.opf("vector", lambda v: v.reduce_sum(rsum[:], prob[:], axis=AX.X))
            em.opf("vector", lambda v: v.reciprocal(rrec[:], rsum[:]))
            em.opf("vector", lambda v: v.tensor_scalar_mul(scst_b[:], prob[:], rrec[:]),
                   inc="v")

            em.wait("tensor", "v")
            em.op("tensor",
                  lambda t: t.transpose(pa.bitcast(BF16)[:, 2048:2048 + 128][0:16, :],
                                        scst_b[:], id128_s[:]),
                  inc="t")
            em.wait("scalar", "t")
            em.op("scalar",
                  lambda s, g=g: s.copy(scores_fmT[:, g * 128:(g + 1) * 128],
                                        pa.bitcast(BF16)[:, 2048:2048 + 128][0:16, :]),
                  inc="s")

        em.wait("sync", "s")
        em.dma("sync", scr_row[:], scores_fmT[:], "d")

        em.op("vector", lambda v: [
            v.memset(acc[:, c * SLAB:(c + 1) * SLAB], 0.0) for c in range(8)][-1],
            inc="v")

        for e in range(E):
            for wsrc, wdst in [(w1[e], w1_b), (w3[e], w3_b), (w2[e], w2_b)]:
                em.wait("sync", "v")
                em.dma("sync", stag[:], fm(wsrc), "d")
                em.wait("vector", "d")

                def cst(v, wdst=wdst):
                    for c in range(4):
                        i = v.tensor_copy(wdst[:, c * 1024:(c + 1) * 1024],
                                          stag[:, c * 1024:(c + 1) * 1024])
                    return i
                em.op("vector", cst, inc="v")

            em.wait("tensor", "v")
            em.wait("tensor", "d")
            em.op("tensor",
                  lambda t, e=e: t.matmul(
                      bank(0)[:, 0:SLAB], ones1x_s[:],
                      scr_row[:, e * SLAB:e * SLAB + SLAB],
                      start=True, stop=True),
                  inc="t")
            em.wait("scalar", "t")
            em.op("scalar", lambda s: s.copy(sb_b[:], bank(0)[:, 0:SLAB]), inc="s")

            em.wait("tensor", "s")

            def gu_mm(t):
                for m in range(4):
                    for k in range(8):
                        t.matmul(bank(m)[:, 0:SLAB],
                                 w1_b[:, k * 512 + m * 128:k * 512 + (m + 1) * 128],
                                 xb[:, k * SLAB:(k + 1) * SLAB],
                                 start=(k == 0), stop=(k == 7))
                for m in range(4):
                    for k in range(8):
                        i = t.matmul(bank(4 + m)[:, 0:SLAB],
                                     w3_b[:, k * 512 + m * 128:k * 512 + (m + 1) * 128],
                                     xb[:, k * SLAB:(k + 1) * SLAB],
                                     start=(k == 0), stop=(k == 7))
                return i
            em.op("tensor", gu_mm, inc="t")

            em.wait("scalar", "t")

            def silu_f(s):
                for m in range(4):
                    i = s.activation(sg[:, m * 512:(m + 1) * 512],
                                     bank(m)[:, 0:SLAB], AF.Silu)
                return i
            em.op("scalar", silu_f, inc="s")

            em.wait("vector", "s")

            def mkact(v):
                for m in range(4):
                    v.tensor_mul(act_b[:, m * 512:(m + 1) * 512],
                                 bank(4 + m)[:, 0:SLAB], sb_b[:])
                    i = v.tensor_mul(act_b[:, m * 512:(m + 1) * 512],
                                     act_b[:, m * 512:(m + 1) * 512],
                                     sg[:, m * 512:(m + 1) * 512])
                return i
            em.op("vector", mkact, inc="v")

            em.wait("tensor", "v")

            def eo_mm(t):
                for m in range(8):
                    for k in range(4):
                        i = t.matmul(
                            bank(m)[:, 0:SLAB],
                            w2_b[:, k * 1024 + m * 128:k * 1024 + (m + 1) * 128],
                            act_b[:, k * 512:(k + 1) * 512],
                            start=(k == 0), stop=(k == 3))
                return i
            em.op("tensor", eo_mm, inc="t")

            em.wait("vector", "t")

            def flush(v):
                for m in range(8):
                    i = v.tensor_add(acc[:, m * SLAB:(m + 1) * SLAB],
                                     acc[:, m * SLAB:(m + 1) * SLAB],
                                     bank(m)[:, 0:SLAB])
                return i
            em.op("vector", flush, inc="v")

        def mk_y(v):
            for c in range(8):
                i = v.tensor_add(y_bf[:, c * SLAB:(c + 1) * SLAB],
                                 acc[:, c * SLAB:(c + 1) * SLAB],
                                 xb[:, c * SLAB:(c + 1) * SLAB])
            return i
        em.op("vector", mk_y, inc="v")

        emit_ln(em, bank, ln_ctx, y_bf, x2_b)
        em.wait("sync", "v")
        em.dma("sync", off_ap(fm(x2)[:, :, 0:SLAB], toff), x2_b[:], "st")

        em.end_iter()
    fori_cm.__exit__(None, None, None)
    if debug:
        dbg_srcs = {"scores_fmT": scores_fmT, "scr_row": scr_row, "sb_b": sb_b,
                    "acc": acc, "lgt": lgt, "prob": prob, "scst_b": scst_b,
                    "idx8f": idx8f, "selm": selm, "y_bf": y_bf,
                    "act_b": act_b, "sg": sg, "px1": px1, "px2": px2,
                    "top8": top8, "nmax": nmax, "rsum": rsum, "rrec": rrec}
        for nm, ap_out in dbg_out.items():
            em.dma("sync", ap_out, dbg_srcs[nm][:], "st")
        em.wait("sync", "st")



def build_l0ffn(T, debug=False):
    nc = _new_nc()
    SLAB = 512
    NSL = T // SLAB
    E = 16

    inp = lambda n, s, d=F32: nc.dram_tensor(n, s, d, kind="ExternalInput").ap()
    x1 = inp("x1", [D, T], BF16)
    gate_wT = inp("gate_wT", [D, E])
    w1 = inp("w1", [E, D, 512])
    w3 = inp("w3", [E, D, 512])
    w2 = inp("w2", [E, 512, D])
    id128 = inp("id128", [128, 128], BF16)
    iota32 = inp("iota32", [128, 32])
    ones128 = inp("ones128", [128, 1], BF16)
    ones1x = inp("ones1x", [1, 128], BF16)

    x2 = nc.dram_tensor("x2", [D, T], BF16, kind="ExternalOutput").ap()
    if debug:
        dbg_out = {}
        for nm, shp, dt_ in [
                ("scores_fmT", [16, 512], BF16), ("scr_row", [1, 16 * 512], BF16),
                ("sb_b", [128, 512], BF16), ("acc", [128, 8 * 512], F32),
                ("lgt", [128, 16], F32), ("prob", [128, 16], F32),
                ("scst_b", [128, 16], BF16), ("idx8f", [128, 8], F32),
                ("px1", [128, 16], F32), ("px2", [128, 16], F32),
                ("top8", [128, 8], F32), ("nmax", [128, 1], F32),
                ("rsum", [128, 1], F32), ("rrec", [128, 1], F32),
                ("selm", [128, 16], F32), ("y_bf", [128, 8 * 512], BF16),
                ("act_b", [128, 4 * 512], BF16), ("sg", [128, 4 * 512], BF16)]:
            dbg_out[nm] = nc.dram_tensor("dbg_" + nm, shp, dt_,
                                         kind="ExternalOutput").ap()

    es = ExitStack()
    em = Em(nc, es)
    pa = es.enter_context(nc.psum_tensor("pa", [128, 4096], F32))
    A = {"x1": x1, "gate_wT": gate_wT, "w1": w1, "w3": w3, "w2": w2, "id128": id128, "iota32": iota32, "ones128": ones128, "ones1x": ones1x, "x2": x2}
    if debug:
        A["dbg_out"] = dbg_out
    emit_l0ffn(nc, em, es, pa, A, T, debug=debug)
    es.close()
    return nc



def emit_l1attn(nc, em, es, pa, A, T, pref="c_", debug=False):
    x2 = A["x2"]
    WqT = A["WqT"]
    bq = A["bq"]
    WkT = A["WkT"]
    bk = A["bk"]
    WvT = A["WvT"]
    bv = A["bv"]
    WoT = A["WoT"]
    bo = A["bo"]
    triuT = A["triuT"]
    id128 = A["id128"]
    ones128 = A["ones128"]
    ones1x = A["ones1x"]
    x3 = A["x3"]
    bank = lambda b: pa[:, 512 * b:512 * (b + 1)]
    bank_bf = lambda b, cols: pa.bitcast(BF16)[:, 1024 * b:1024 * b + cols]
    CH = 128
    NCH = T // CH
    sb = lambda n, s, d: es.enter_context(nc.sbuf_tensor(pref + n, s, d))

    wq_b = sb("wq_b", [128, 8 * 1024], BF16)
    wk_b = sb("wk_b", [128, 8 * 1024], BF16)
    wv_b = sb("wv_b", [128, 8 * 1024], BF16)
    wo_b = sb("wo_b", [128, 8 * 1024], BF16)
    stag = sb("stag", [128, 4 * 1024], F32)
    biasq = sb("biasq", [128, 8], F32)
    biask = sb("biask", [128, 8], F32)
    biasv = sb("biasv", [128, 8], F32)
    biaso = sb("biaso", [128, 8], F32)
    triu_s = sb("triu_s", [128, 128], BF16)
    id128_s = sb("id128_s", [128, 128], BF16)
    ones_s = sb("ones_s", [128, 1], BF16)
    ones1x_s = sb("ones1x_s", [1, 128], BF16)

    xb = sb("xb", [128, 8 * CH], BF16)
    q_fm = sb("q_fm", [128, 8 * CH], BF16)
    k_fm = sb("k_fm", [128, 8 * CH], BF16)
    v_fm = sb("v_fm", [128, 8 * CH], BF16)
    k_tm = sb("k_tm", [128, 8 * CH], BF16)
    v_tm = sb("v_tm", [128, 8 * CH], BF16)
    t1 = sb("t1", [128, CH], F32)
    t2 = sb("t2", [128, CH], F32)
    te = sb("te", [128, CH], F32)
    smT = sb("smT", [128, 128], BF16)
    dr = sb("dr", [128, 1], F32)
    sz_t = sb("sz_t", [128, 8 * 65], F32)
    sz_b = sb("sz_b", [128, 8 * 65], BF16)
    attn_tm = sb("attn_tm", [128, NH * HD], BF16)
    attn_fm = sb("attn_fm", [128, 8 * CH], BF16)
    y_bf = sb("y_bf", [128, 8 * CH], BF16)
    y2_bf = sb("y2_bf", [128, CH], BF16)
    srow = sb("srow", [1, 4 * CH], F32)
    rstd_r = sb("rstd_r", [1, CH], BF16)
    shift_r = sb("shift_r", [1, CH], BF16)
    x3_b = sb("x3_b", [128, 8 * CH], BF16)


    for dst, srcp in [(biasq, col(bq)), (biask, col(bk)), (biasv, col(bv)),
                      (biaso, col(bo))]:
        em.dma("sync", dst[:], srcp, "d", allow_slow_non_contiguous=True)
    for dst, srcp in [(triu_s, triuT), (id128_s, id128), (ones_s, ones128),
                      (ones1x_s, ones1x)]:
        em.dma("sync", dst[:], srcp, "d")

    def cast_pass(dst_off, dst, n):
        def f(v):
            for c in range(n):
                i = v.tensor_copy(dst[:, dst_off + c * 1024:dst_off + (c + 1) * 1024],
                                  stag[:, c * 1024:(c + 1) * 1024])
            return i
        return f

    for w_ap, dst in [(WqT, wq_b), (WkT, wk_b), (WvT, wv_b), (WoT, wo_b)]:
        view = fm(w_ap)
        for k0 in range(0, 8, 4):
            em.wait("sync", "v")
            em.dma("sync", stag[:], view[:, k0:k0 + 4, :], "d")
            em.wait("vector", "d")
            em.op("vector", cast_pass(k0 * 1024, dst, 4), inc="v")

    def init_sz(v):
        v.memset(sz_t[:], 0.0)
        return v.memset(sz_b[:], 0.0)
    em.op("vector", init_sz, inc="v")

    ln_ctx = {"ones_s": ones_s, "ones1x_s": ones1x_s, "y2_bf": y2_bf,
              "srow": srow, "rstd_r": rstd_r, "shift_r": shift_r}

    def proj(name, wsb, biast, dst, do_phi):
        def mm(t, wsb=wsb):
            for m in range(8):
                bk_ = m // 4
                for k in range(8):
                    i = t.matmul(
                        bank(bk_)[:, (m % 4) * 128:(m % 4 + 1) * 128],
                        wsb[:, k * 1024 + m * 128:k * 1024 + (m + 1) * 128],
                        xb[:, k * CH:(k + 1) * CH],
                        start=(k == 0 and m % 4 == 0),
                        stop=(k == 7 and m % 4 == 3))
            return i
        em.op("tensor", mm, inc="t")
        for m in range(8):
            src = lambda m=m: bank(m // 4)[:, (m % 4) * 128:(m % 4 + 1) * 128]
            if do_phi:
                em.wait("vector", "t")
                em.op("vector",
                      lambda v, m=m, src=src: v.tensor_scalar(
                          t1[:], src(), biast[:, m:m + 1], 0.0,
                          op0=ALU.add, op1=ALU.min),
                      inc="v")
                em.wait("scalar", "v")
                em.op("scalar",
                      lambda s: s.activation(te[:], t1[:], AF.Exp), inc="s")
                em.wait("vector", "s")

                def phi_fin(v, m=m, src=src):
                    v.tensor_scalar(t2[:], src(), biast[:, m:m + 1], 0.0,
                                    op0=ALU.add, op1=ALU.max)
                    i = v.tensor_add(dst[:, m * CH:(m + 1) * CH], t2[:], te[:])
                    return i
                em.op("vector", phi_fin, inc="v")
            else:
                em.wait("vector", "t")
                em.op("vector",
                      lambda v, m=m, src=src: v.tensor_scalar_add(
                          dst[:, m * CH:(m + 1) * CH], src(), biast[:, m:m + 1]),
                      inc="v")

    em.declare(["d", "v", "t", "s", "g", "st"])
    em.loop_prologue()
    fori_cm = nc.Fori(0, NCH)
    ci_r = fori_cm.__enter__()
    if True:
        toff = nc.sync.compute_val(ci_r * CH)
        em.dma("sync", xb[:], off_ap(fm(x2)[:, :, 0:CH], toff), "d")
        em.wait("tensor", "d")
        em.wait("tensor", "v")
        em.wait("tensor", "s")

        proj("q", wq_b, biasq, q_fm, True)
        em.wait("tensor", "v")
        proj("k", wk_b, biask, k_fm, True)
        em.wait("tensor", "v")
        proj("v", wv_b, biasv, v_fm, False)

        em.wait("tensor", "v")
        for w2_ in range(2):
            for i4 in range(4):
                c = w2_ * 4 + i4
                em.op("tensor",
                      lambda t, c=c, i4=i4: t.transpose(
                          bank_bf(6, 1024)[:, i4 * 128:(i4 + 1) * 128],
                          k_fm[:, c * CH:(c + 1) * CH], id128_s[:]),
                      inc="t")
                em.op("tensor",
                      lambda t, c=c, i4=i4: t.transpose(
                          bank_bf(7, 1024)[:, i4 * 128:(i4 + 1) * 128],
                          v_fm[:, c * CH:(c + 1) * CH], id128_s[:]),
                      inc="t")
            em.wait("scalar", "t")

            def tev(s, w2_=w2_):
                for i4 in range(4):
                    c = w2_ * 4 + i4
                    s.copy(k_tm[:, c * CH:(c + 1) * CH],
                           bank_bf(6, 1024)[:, i4 * 128:(i4 + 1) * 128])
                    i = s.copy(v_tm[:, c * CH:(c + 1) * CH],
                               bank_bf(7, 1024)[:, i4 * 128:(i4 + 1) * 128])
                return i
            em.op("scalar", tev, inc="s")
            if w2_ == 0:
                em.wait("tensor", "s")

        for h in range(NH):
            hb = (h % 2) * 64
            hc = h // 2
            qh = q_fm[hb:hb + 64, hc * CH:(hc + 1) * CH]
            kh = k_fm[hb:hb + 64, hc * CH:(hc + 1) * CH]
            szh = sz_t[hb:hb + 64, (h // 2) * 65:(h // 2 + 1) * 65]
            szh_b = sz_b[hb:hb + 64, (h // 2) * 65:(h // 2 + 1) * 65]

            em.wait("tensor", "s")
            em.wait("tensor", "v")
            em.op("tensor",
                  lambda t, kh=kh, qh=qh: t.matmul(
                      bank(0)[:, 0:128], kh, qh, start=True, stop=True),
                  inc="t")
            em.wait("vector", "t")
            em.op("vector",
                  lambda v: v.tensor_mul(smT[:], bank(0)[:, 0:128], triu_s[:]),
                  inc="v")

            em.wait("tensor", "v")

            def num_mm(t, h=h, qh=qh, szh_b=szh_b):
                t.matmul(bank(1)[:, 0:64], smT[:],
                         v_tm[:, h * 64:(h + 1) * 64], start=True, stop=False)
                t.matmul(bank(1)[:, 64:65], smT[:], ones_s[:],
                         start=False, stop=False)
                i = t.matmul(bank(1)[:, 0:65], qh, szh_b, start=False, stop=True)
                return i
            em.op("tensor", num_mm, inc="t")

            em.wait("vector", "t")
            em.opf("vector",
                   lambda v: v.reciprocal(dr[:], bank(1)[:, 64:65]), inc="v")
            em.wait("scalar", "v")
            em.op("scalar",
                  lambda s, h=h: s.activation(
                      attn_tm[:, h * 64:(h + 1) * 64], bank(1)[:, 0:64],
                      AF.Copy, scale=dr[:]),
                  inc="s")

            def delta_mm(t, h=h, hb=hb):
                t.matmul(bank(2)[hb:hb + 64, 0:64],
                         k_tm[:, h * 64:(h + 1) * 64],
                         v_tm[:, h * 64:(h + 1) * 64], start=True, stop=False)
                i = t.matmul(bank(2)[hb:hb + 64, 64:65],
                             k_tm[:, h * 64:(h + 1) * 64], ones_s[:],
                             start=False, stop=True)
                return i
            em.op("tensor", delta_mm, inc="t")

            em.wait("vector", "t")

            em.op("vector",
                  lambda v, szh=szh, hb=hb: v.tensor_add(
                      szh, szh, bank(2)[hb:hb + 64, 0:65]))
            em.opf("vector",
                   lambda v, szh=szh, szh_b=szh_b: v.tensor_copy(szh_b, szh),
                   inc="v")

        em.wait("tensor", "s")
        em.wait("tensor", "v")
        for w2_ in range(2):
            for i4 in range(4):
                c = w2_ * 4 + i4
                em.op("tensor",
                      lambda t, c=c, i4=i4: t.transpose(
                          bank_bf(6, 1024)[:, i4 * 128:(i4 + 1) * 128],
                          attn_tm[:, c * CH:(c + 1) * CH], id128_s[:]),
                      inc="t")
            em.wait("scalar", "t")

            def tev2(s, w2_=w2_):
                for i4 in range(4):
                    c = w2_ * 4 + i4
                    i = s.copy(attn_fm[:, c * CH:(c + 1) * CH],
                               bank_bf(6, 1024)[:, i4 * 128:(i4 + 1) * 128])
                return i
            em.op("scalar", tev2, inc="s")
            if w2_ == 0:
                em.wait("tensor", "s")

        em.wait("tensor", "s")

        def wo_mm(t):
            for m in range(8):
                bk_ = m // 4
                for k in range(8):
                    i = t.matmul(
                        bank(bk_)[:, (m % 4) * 128:(m % 4 + 1) * 128],
                        wo_b[:, k * 1024 + m * 128:k * 1024 + (m + 1) * 128],
                        attn_fm[:, k * CH:(k + 1) * CH],
                        start=(k == 0 and m % 4 == 0),
                        stop=(k == 7 and m % 4 == 3))
            return i
        em.op("tensor", wo_mm, inc="t")

        em.wait("vector", "t")

        def mk_y(v):
            for m in range(8):
                i = v.scalar_tensor_tensor(
                    y_bf[:, m * CH:(m + 1) * CH],
                    bank(m // 4)[:, (m % 4) * 128:(m % 4 + 1) * 128],
                    biaso[:, m:m + 1], xb[:, m * CH:(m + 1) * CH],
                    op0=ALU.add, op1=ALU.add)
            return i
        em.op("vector", mk_y, inc="v")

        emit_ln(em, bank, ln_ctx, y_bf, x3_b)
        em.wait("sync", "v")
        em.dma("sync", off_ap(fm(x3)[:, :, 0:CH], toff), x3_b[:], "st")

        em.end_iter()
    fori_cm.__exit__(None, None, None)



def build_l1attn(T, debug=False):
    nc = _new_nc()
    CH = 128
    NCH = T // CH

    inp = lambda n, s, d=F32: nc.dram_tensor(n, s, d, kind="ExternalInput").ap()
    x2 = inp("x2", [D, T], BF16)
    WqT = inp("WqT", [D, D]); bq = inp("bq", [D])
    WkT = inp("WkT", [D, D]); bk = inp("bk", [D])
    WvT = inp("WvT", [D, D]); bv = inp("bv", [D])
    WoT = inp("WoT", [D, D]); bo = inp("bo", [D])
    triuT = inp("triuT", [128, 128], BF16)
    id128 = inp("id128", [128, 128], BF16)
    ones128 = inp("ones128", [128, 1], BF16)
    ones1x = inp("ones1x", [1, 128], BF16)

    x3 = nc.dram_tensor("x3", [D, T], BF16, kind="ExternalOutput").ap()

    es = ExitStack()
    em = Em(nc, es)
    pa = es.enter_context(nc.psum_tensor("pa", [128, 4096], F32))
    A = {"x2": x2, "WqT": WqT, "bq": bq, "WkT": WkT, "bk": bk, "WvT": WvT, "bv": bv, "WoT": WoT, "bo": bo, "triuT": triuT, "id128": id128, "ones128": ones128, "ones1x": ones1x, "x3": x3}
    emit_l1attn(nc, em, es, pa, A, T, debug=debug)
    es.close()
    return nc



def emit_l1ffn(nc, em, es, pa, A, T, pref="d_", debug=False):
    x3 = A["x3"]
    gate_wT = A["gate_wT"]
    e_bias = A["e_bias"]
    guT = A["guT"]
    dnT = A["dnT"]
    sgT = A["sgT"]
    suT = A["suT"]
    sdT = A["sdT"]
    id128 = A["id128"]
    iota32 = A["iota32"]
    ones128 = A["ones128"]
    ones1x = A["ones1x"]
    ones1xf = A["ones1xf"]
    x4 = A["x4"]
    bank = lambda b: pa[:, 512 * b:512 * (b + 1)]
    bank_bf = lambda b, cols: pa.bitcast(BF16)[:, 1024 * b:1024 * b + cols]
    SLAB = 512
    NSL = T // SLAB
    E = 32
    sb = lambda n, s, d: es.enter_context(nc.sbuf_tensor(pref + n, s, d))

    xb = sb("xb", [128, 8 * SLAB], BF16)
    gw_b = sb("gw_b", [128, 8 * E], BF16)
    gu_b = sb("gu_b", [128, 8 * 1024], BF16)
    dn_b = sb("dn_b", [128, 4 * 1024], BF16)
    stag = sb("stag", [128, 4 * 1024], F32)
    acc = sb("acc", [128, 8 * SLAB], F32)
    sg = sb("sg", [128, 4 * 512], BF16)
    act_b = sb("act_b", [128, 4 * 512], BF16)
    sb_b = sb("sb_b", [128, SLAB], BF16)
    lg_b = sb("lg_b", [32, SLAB], BF16)
    lgt = sb("lgt", [128, 32], F32)
    scores = sb("scores", [128, 32], F32)
    sfc = sb("sfc", [128, 32], F32)
    gsc = sb("gsc", [128, 8], F32)
    gmask = sb("gmask", [128, 8], F32)
    masked = sb("masked", [128, 32], F32)
    selm = sb("selm", [128, 32], F32)
    ohot = sb("ohot", [128, 32], F32)
    scst_b = sb("scst_b", [128, 32], BF16)
    top8 = sb("top8", [128, 8], F32)
    idx8u = sb("idx8u", [128, 8], U32)
    idx8f = sb("idx8f", [128, 8], F32)
    rsum = sb("rsum", [128, 1], F32)
    rrec = sb("rrec", [128, 1], F32)
    ebias_row = sb("ebias_row", [1, E], F32)
    ebias_b = sb("ebias_b", [128, E], F32)
    scores_fmT = sb("scores_fmT", [32, SLAB], BF16)
    scr_row = sb("scr_row", [1, E * SLAB], BF16)
    id128_s = sb("id128_s", [128, 128], BF16)
    iota_s = sb("iota_s", [128, 32], F32)
    ones_s = sb("ones_s", [128, 1], BF16)
    ones1x_s = sb("ones1x_s", [1, 128], BF16)
    ones1xf_s = sb("ones1xf_s", [1, 128], F32)
    y_bf = sb("y_bf", [128, 8 * SLAB], BF16)
    y2_bf = sb("y2_bf", [128, SLAB], BF16)
    srow = sb("srow", [1, 4 * SLAB], F32)
    rstd_r = sb("rstd_r", [1, SLAB], BF16)
    shift_r = sb("shift_r", [1, SLAB], BF16)
    x4_f = sb("x4_f", [128, 8 * SLAB], F32)


    for dst, srcp in [(id128_s, id128), (iota_s, iota32), (ones_s, ones128),
                      (ones1x_s, ones1x), (ones1xf_s, ones1xf)]:
        em.dma("sync", dst[:], srcp, "d")
    em.dma("sync", ebias_row[:], e_bias.rearrange("(a b) -> a b", a=1), "d")
    em.dma("sync", stag[:, 0:E * 8], fm(gate_wT), "d")
    em.wait("vector", "d")
    em.op("vector", lambda v: v.tensor_copy(gw_b[:], stag[:, 0:E * 8]), inc="v")
    em.wait("tensor", "d")
    em.op("tensor",
          lambda t: t.matmul(bank(0)[:, 0:E], ones1xf_s[:], ebias_row[:],
                             start=True, stop=True),
          inc="t")
    em.wait("scalar", "t")
    em.op("scalar", lambda s: s.copy(ebias_b[:], bank(0)[:, 0:E]), inc="s")

    ln_ctx = {"ones_s": ones_s, "ones1x_s": ones1x_s, "y2_bf": y2_bf,
              "srow": srow, "rstd_r": rstd_r, "shift_r": shift_r}

    def cast_pass(dst_off, dst, n):
        def f(v):
            for c in range(n):
                i = v.tensor_copy(dst[:, dst_off + c * 1024:dst_off + (c + 1) * 1024],
                                  stag[:, c * 1024:(c + 1) * 1024])
            return i
        return f

    def stream_w(w_ap, dst, nk, koff=0):
        view = fm(w_ap)
        M_ = w_ap.shape[-1]
        per = 4096 // M_
        k0 = 0
        while k0 < nk:
            n = min(per, nk - k0)
            em.wait("sync", "v")
            em.dma("sync", stag[:, 0:n * M_], view[:, koff + k0:koff + k0 + n, :], "d")
            em.wait("vector", "d")
            em.op("vector", cast_pass(k0 * M_, dst, n * M_ // 1024), inc="v")
            k0 += n

    def expert_mms(lh_g, lh_u, lh_dn, use_score):
        em.wait("tensor", "s" if use_score else "v")
        em.wait("tensor", "v")

        def gu_mm(t):
            for m in range(4):
                for k in range(8):
                    t.matmul(bank(m)[:, 0:SLAB], lh_g(k, m),
                             xb[:, k * SLAB:(k + 1) * SLAB],
                             start=(k == 0), stop=(k == 7))
            for m in range(4):
                for k in range(8):
                    i = t.matmul(bank(4 + m)[:, 0:SLAB], lh_u(k, m),
                                 xb[:, k * SLAB:(k + 1) * SLAB],
                                 start=(k == 0), stop=(k == 7))
            return i
        em.op("tensor", gu_mm, inc="t")

        em.wait("scalar", "t")

        def silu_f(s):
            for m in range(4):
                i = s.activation(sg[:, m * 512:(m + 1) * 512],
                                 bank(m)[:, 0:SLAB], AF.Silu)
            return i
        em.op("scalar", silu_f, inc="s")

        em.wait("vector", "s")

        def mkact(v):
            for m in range(4):
                if use_score:
                    v.tensor_mul(act_b[:, m * 512:(m + 1) * 512],
                                 bank(4 + m)[:, 0:SLAB], sb_b[:])
                    i = v.tensor_mul(act_b[:, m * 512:(m + 1) * 512],
                                     act_b[:, m * 512:(m + 1) * 512],
                                     sg[:, m * 512:(m + 1) * 512])
                else:
                    i = v.tensor_mul(act_b[:, m * 512:(m + 1) * 512],
                                     bank(4 + m)[:, 0:SLAB],
                                     sg[:, m * 512:(m + 1) * 512])
            return i
        em.op("vector", mkact, inc="v")

        em.wait("tensor", "v")

        def eo_mm(t):
            for m in range(8):
                for k in range(4):
                    i = t.matmul(bank(m)[:, 0:SLAB], lh_dn(k, m),
                                 act_b[:, k * 512:(k + 1) * 512],
                                 start=(k == 0), stop=(k == 3))
            return i
        em.op("tensor", eo_mm, inc="t")

        em.wait("vector", "t")

        def flush(v):
            for m in range(8):
                i = v.tensor_add(acc[:, m * SLAB:(m + 1) * SLAB],
                                 acc[:, m * SLAB:(m + 1) * SLAB],
                                 bank(m)[:, 0:SLAB])
            return i
        em.op("vector", flush, inc="v")

    em.declare(["d", "v", "t", "s", "g", "st"])
    em.loop_prologue()
    fori_cm = nc.Fori(0, NSL)
    sl_r = fori_cm.__enter__()
    if True:
        toff = nc.sync.compute_val(sl_r * SLAB)
        em.dma("sync", xb[:], off_ap(fm(x3)[:, :, 0:SLAB], toff), "d")

        em.wait("tensor", "d")
        em.wait("tensor", "v")
        em.wait("tensor", "s")

        def lg_mm(t):
            for k in range(8):
                i = t.matmul(bank(0)[0:32, 0:SLAB], gw_b[:, k * E:(k + 1) * E],
                             xb[:, k * SLAB:(k + 1) * SLAB],
                             start=(k == 0), stop=(k == 7))
            return i
        em.op("tensor", lg_mm, inc="t")

        em.wait("scalar", "t")
        em.op("scalar", lambda s: s.copy(lg_b[:], bank(0)[0:32, 0:SLAB]), inc="s")

        for g in range(SLAB // 128):
            em.wait("tensor", "s")
            em.wait("tensor", "v")
            em.op("tensor",
                  lambda t, g=g: t.transpose(
                      pa.bitcast(BF16)[:, 1024:1024 + 32],
                      lg_b[:, g * 128:(g + 1) * 128], id128_s[0:32, 0:32]),
                  inc="t")
            em.wait("scalar", "t")
            em.op("scalar",
                  lambda s: s.activation(scores[:],
                                         pa.bitcast(BF16)[:, 1024:1024 + 32],
                                         AF.Sigmoid),
                  inc="s")
            em.wait("vector", "s")

            em.opf("vector", lambda v: v.tensor_add(sfc[:], scores[:], ebias_b[:]))
            em.opf("vector", lambda v: v.reduce_max(
                gsc[:], sfc[:].rearrange("p (a b) -> p a b", a=8), axis=AX.X))
            em.opf("vector", lambda v: v.max(top8[:], gsc[:]))
            em.opf("vector", lambda v: v.max_index(idx8u[:], top8[:], gsc[:]))
            em.opf("vector", lambda v: v.tensor_copy(idx8f[:], idx8u[:]))
            em.opf("vector", lambda v: v.memset(gmask[:], 0.0))
            for kk in range(4):
                em.opf("vector",
                       lambda v, kk=kk: v.tensor_single_scalar(
                           ohot[:, 0:8], iota_s[:, 0:8], idx8f[:, kk:kk + 1],
                           op=ALU.is_equal))
                em.opf("vector",
                       lambda v: v.tensor_add(gmask[:], gmask[:], ohot[:, 0:8]))
            em.opf("vector", lambda v: v.tensor_mul(
                masked[:].rearrange("p (a b) -> p a b", a=8),
                sfc[:].rearrange("p (a b) -> p a b", a=8),
                gmask[:, :, None].broadcast_to([128, 8, 4])), inc="v")

            em.opf("vector", lambda v: v.max(top8[:], masked[:]))
            em.opf("vector", lambda v: v.max_index(idx8u[:], top8[:], masked[:]))
            em.opf("vector", lambda v: v.tensor_copy(idx8f[:], idx8u[:]))
            em.opf("vector", lambda v: v.memset(selm[:], 0.0))
            for kk in range(6):
                em.opf("vector",
                       lambda v, kk=kk: v.tensor_single_scalar(
                           ohot[:], iota_s[:], idx8f[:, kk:kk + 1],
                           op=ALU.is_equal))
                em.opf("vector",
                       lambda v: v.tensor_add(selm[:], selm[:], ohot[:]))
            em.opf("vector", lambda v: v.tensor_mul(selm[:], selm[:], scores[:]))
            em.opf("vector", lambda v: v.reduce_sum(rsum[:], selm[:], axis=AX.X))
            em.opf("vector", lambda v: v.tensor_scalar_add(rsum[:], rsum[:], 1e-20))
            em.opf("vector", lambda v: v.reciprocal(rrec[:], rsum[:]))
            em.opf("vector", lambda v: v.tensor_scalar(scst_b[:], selm[:], rrec[:],
                                                       2.5, op0=ALU.mult, op1=ALU.mult),
                   inc="v")

            em.wait("tensor", "v")
            em.op("tensor",
                  lambda t: t.transpose(pa.bitcast(BF16)[:, 2048:2048 + 128][0:32, :],
                                        scst_b[:], id128_s[:]),
                  inc="t")
            em.wait("scalar", "t")
            em.op("scalar",
                  lambda s, g=g: s.copy(scores_fmT[:, g * 128:(g + 1) * 128],
                                        pa.bitcast(BF16)[:, 2048:2048 + 128][0:32, :]),
                  inc="s")

        em.wait("sync", "s")
        em.dma("sync", scr_row[:], scores_fmT[:], "d")

        em.op("vector", lambda v: [
            v.memset(acc[:, c * SLAB:(c + 1) * SLAB], 0.0) for c in range(8)][-1],
            inc="v")

        for half in range(2):
            em.wait("sync", "v")
            em.dma("sync", stag[:],
                   fm(sgT)[:, :, half * 512:(half + 1) * 512], "d")
            em.wait("vector", "d")
            em.op("vector", cast_pass(0, gu_b, 4), inc="v")
            em.wait("sync", "v")
            em.dma("sync", stag[:],
                   fm(suT)[:, :, half * 512:(half + 1) * 512], "d")
            em.wait("vector", "d")
            em.op("vector", cast_pass(4096, gu_b, 4), inc="v")
            stream_w(sdT, dn_b, 4, koff=half * 4)

            expert_mms(
                lambda k, m: gu_b[:, k * 512 + m * 128:k * 512 + (m + 1) * 128],
                lambda k, m: gu_b[:, 4096 + k * 512 + m * 128:4096 + k * 512 + (m + 1) * 128],
                lambda k, m: dn_b[:, k * 1024 + m * 128:k * 1024 + (m + 1) * 128],
                use_score=False)

        for e in range(E):
            stream_w(guT[e], gu_b, 8)
            stream_w(dnT[e], dn_b, 4)

            em.wait("tensor", "v")
            em.wait("tensor", "d")
            em.op("tensor",
                  lambda t, e=e: t.matmul(
                      bank(0)[:, 0:SLAB], ones1x_s[:],
                      scr_row[:, e * SLAB:e * SLAB + SLAB],
                      start=True, stop=True),
                  inc="t")
            em.wait("scalar", "t")
            em.op("scalar", lambda s: s.copy(sb_b[:], bank(0)[:, 0:SLAB]), inc="s")

            expert_mms(
                lambda k, m: gu_b[:, k * 1024 + m * 128:k * 1024 + (m + 1) * 128],
                lambda k, m: gu_b[:, k * 1024 + 512 + m * 128:k * 1024 + 512 + (m + 1) * 128],
                lambda k, m: dn_b[:, k * 1024 + m * 128:k * 1024 + (m + 1) * 128],
                use_score=True)

        def mk_y(v):
            for c in range(8):
                i = v.tensor_add(y_bf[:, c * SLAB:(c + 1) * SLAB],
                                 acc[:, c * SLAB:(c + 1) * SLAB],
                                 xb[:, c * SLAB:(c + 1) * SLAB])
            return i
        em.op("vector", mk_y, inc="v")

        emit_ln(em, bank, ln_ctx, y_bf, x4_f)
        em.wait("sync", "v")
        em.dma("sync", off_ap(fm(x4)[:, :, 0:SLAB], toff), x4_f[:], "st")

        em.end_iter()
    fori_cm.__exit__(None, None, None)



def build_l1ffn(T, debug=False):
    nc = _new_nc()
    SLAB = 512
    NSL = T // SLAB
    E = 32

    inp = lambda n, s, d=F32: nc.dram_tensor(n, s, d, kind="ExternalInput").ap()
    x3 = inp("x3", [D, T], BF16)
    gate_wT = inp("gate_wT", [D, E])
    e_bias = inp("e_bias", [E])
    guT = inp("guT", [E, D, D])
    dnT = inp("dnT", [E, 512, D])
    sgT = inp("sgT", [D, D])
    suT = inp("suT", [D, D])
    sdT = inp("sdT", [D, D])
    id128 = inp("id128", [128, 128], BF16)
    iota32 = inp("iota32", [128, 32])
    ones128 = inp("ones128", [128, 1], BF16)
    ones1x = inp("ones1x", [1, 128], BF16)
    ones1xf = inp("ones1xf", [1, 128])

    x4 = nc.dram_tensor("x4", [D, T], F32, kind="ExternalOutput").ap()

    es = ExitStack()
    em = Em(nc, es)
    pa = es.enter_context(nc.psum_tensor("pa", [128, 4096], F32))
    A = {"x3": x3, "gate_wT": gate_wT, "e_bias": e_bias, "guT": guT, "dnT": dnT, "sgT": sgT, "suT": suT, "sdT": sdT, "id128": id128, "iota32": iota32, "ones128": ones128, "ones1x": ones1x, "ones1xf": ones1xf, "x4": x4}
    emit_l1ffn(nc, em, es, pa, A, T, debug=debug)
    es.close()
    return nc




def build_fused(T):
    nc = _new_nc()
    es = ExitStack()
    em = Em(nc, es)
    pa = es.enter_context(nc.psum_tensor("pa", [128, 4096], F32))
    inp = lambda n, s, d=F32: nc.dram_tensor(n, s, d, kind="ExternalInput").ap()
    mid = lambda n: nc.dram_tensor(n, [D, T], BF16, kind="Internal").ap()

    id128 = inp("id128", [128, 128], BF16)
    id64x2 = inp("id64x2", [128, 64], BF16)
    ones128 = inp("ones128", [128, 1], BF16)
    ones1x = inp("ones1x", [1, 128], BF16)
    ones1xf = inp("ones1xf", [1, 128])
    iota32 = inp("iota32", [128, 32])

    x1 = mid("x1")
    x2 = mid("x2")
    x3 = mid("x3")
    x4 = nc.dram_tensor("x4", [D, T], F32, kind="ExternalOutput").ap()

    A1 = {
        "xT": inp("xT", [D, T]),
        "Wq": inp("Wq", [D, D]), "bq": inp("bq", [D]),
        "Wdown": inp("Wdown", [D, 256]), "bdown": inp("bdown", [256]),
        "Wup": inp("Wup", [256, 2048]), "bup": inp("bup", [2048]),
        "Wout": inp("Wout", [D, D]), "bout": inp("bout", [D]),
        "ropeC": inp("ropeC", [128, T], BF16),
        "ropeS": inp("ropeS", [128, T], BF16),
        "pswap": inp("pswap", [128, 128], BF16),
        "selhi": inp("selhi", [128, 64], BF16),
        "selup": inp("selup", [64, 128], BF16),
        "maskL": inp("maskL", [16, 128], BF16),
        "maskR": inp("maskR", [16, 128], BF16),
        "id128": id128, "id64x2": id64x2, "ones128": ones128,
        "ones1x": ones1x, "x1": x1,
    }
    A2 = {
        "x1": x1,
        "gate_wT": inp("b_gate_wT", [D, 16]),
        "w1": inp("b_w1", [16, D, 512]),
        "w3": inp("b_w3", [16, D, 512]),
        "w2": inp("b_w2", [16, 512, D]),
        "id128": id128, "iota32": iota32, "ones128": ones128,
        "ones1x": ones1x, "x2": x2,
    }
    A3 = {
        "x2": x2,
        "WqT": inp("c_WqT", [D, D]), "bq": inp("c_bq", [D]),
        "WkT": inp("c_WkT", [D, D]), "bk": inp("c_bk", [D]),
        "WvT": inp("c_WvT", [D, D]), "bv": inp("c_bv", [D]),
        "WoT": inp("c_WoT", [D, D]), "bo": inp("c_bo", [D]),
        "triuT": inp("triuT", [128, 128], BF16),
        "id128": id128, "ones128": ones128, "ones1x": ones1x, "x3": x3,
    }
    A4 = {
        "x3": x3,
        "gate_wT": inp("d_gate_wT", [D, 32]),
        "e_bias": inp("d_e_bias", [32]),
        "guT": inp("d_guT", [32, D, D]),
        "dnT": inp("d_dnT", [32, 512, D]),
        "sgT": inp("d_sgT", [D, D]),
        "suT": inp("d_suT", [D, D]),
        "sdT": inp("d_sdT", [D, D]),
        "id128": id128, "iota32": iota32, "ones128": ones128,
        "ones1x": ones1x, "ones1xf": ones1xf, "x4": x4,
    }

    with ExitStack() as es1:
        emit_l0attn(nc, em, es1, pa, A1, T)
    with ExitStack() as es2:
        emit_l0ffn(nc, em, es2, pa, A2, T)
    with ExitStack() as es3:
        emit_l1attn(nc, em, es3, pa, A3, T)
    with ExitStack() as es4:
        emit_l1ffn(nc, em, es4, pa, A4, T)

    es.close()
    return nc


def kernel(**inputs):
    T = inputs["L0_attn_x"].shape[1]
    cores = list(range(NCORES))
    c = make_consts(T)
    x = inputs["L0_attn_x"]

    nc = build_fused(T)
    common = {
        "id128": c["id128"], "id64x2": c["id64x2"], "ones128": c["ones128"],
        "ones1x": c["ones1x"], "ones1xf": c["ones1xf"], "iota32": c["iota32"],
        "Wq": inputs["L0_attn_Wq"], "bq": inputs["L0_attn_bq"],
        "Wdown": inputs["L0_attn_Wdown"], "bdown": inputs["L0_attn_bdown"],
        "Wup": inputs["L0_attn_Wup"], "bup": inputs["L0_attn_bup"],
        "Wout": inputs["L0_attn_Wout"], "bout": inputs["L0_attn_bout"],
        "ropeC": c["ropeC"], "ropeS": c["ropeS"], "pswap": c["pswap"],
        "selhi": c["selhi"], "selup": c["selup"], "maskL": c["maskL"],
        "maskR": c["maskR"],
        "b_gate_wT": np.ascontiguousarray(inputs["L0_ffn_gate_w"].T),
        "b_w1": inputs["L0_ffn_w1"], "b_w3": inputs["L0_ffn_w3"],
        "b_w2": inputs["L0_ffn_w2"],
        "c_WqT": np.ascontiguousarray(inputs["L1_attn_Wq"].T),
        "c_bq": inputs["L1_attn_bq"],
        "c_WkT": np.ascontiguousarray(inputs["L1_attn_Wk"].T),
        "c_bk": inputs["L1_attn_bk"],
        "c_WvT": np.ascontiguousarray(inputs["L1_attn_Wv"].T),
        "c_bv": inputs["L1_attn_bv"],
        "c_WoT": np.ascontiguousarray(inputs["L1_attn_Wo"].T),
        "c_bo": inputs["L1_attn_bo"],
        "triuT": c["triuT"],
        "d_gate_wT": np.ascontiguousarray(inputs["L1_ffn_gate_weight"].T),
        "d_e_bias": inputs["L1_ffn_e_score_correction_bias"],
        "d_guT": np.ascontiguousarray(
            inputs["L1_ffn_experts_gate_up"].transpose(0, 2, 1)),
        "d_dnT": np.ascontiguousarray(
            inputs["L1_ffn_experts_down"].transpose(0, 2, 1)),
        "d_sgT": np.ascontiguousarray(inputs["L1_ffn_shared_gate_w"].T),
        "d_suT": np.ascontiguousarray(inputs["L1_ffn_shared_up_w"].T),
        "d_sdT": np.ascontiguousarray(inputs["L1_ffn_shared_down_w"].T),
    }
    in_maps = [dict(common, xT=np.ascontiguousarray(x[b].T))
               for b in range(NCORES)]
    res = run_bass_kernel_spmd(nc, in_maps, core_ids=cores)
    out = np.stack([np.asarray(r["x4"]).T for r in res.results], axis=0)
    return np.ascontiguousarray(out)
